# version 01
# baseline (speedup 1.0000x reference)
import numpy as np
import concourse.bass as bass
import concourse.mybir as mybir
from concourse.bass_types import AP
from concourse.bass_core_rust import IndirectOffsetOnAxis
from concourse.bass_utils import run_bass_kernel_spmd

F32 = mybir.dt.float32
BF16 = mybir.dt.bfloat16
I32 = mybir.dt.int32
AF = mybir.ActivationFunctionType
ALU = mybir.AluOpType
AX = mybir.AxisListType

ENGINES = ("sync", "gpsimd", "scalar", "vector", "tensor")

SEQ = 4096
LNEPS = 1e-5
RMSEPS = 1e-5


class SB:

    def __init__(self):
        self.ops = {e: [] for e in ENGINES}
        self.cnt = {e: 0 for e in ENGINES}
        self.last = None

    def emit(self, eng, fn, deps=None, dma=False, chain=True):
        if deps is None:
            deps = [self.last] if self.last is not None else []
        else:
            deps = [d for d in deps if d is not None]
        inc = 16 if dma else 1
        waits = {}
        for (e, v) in deps:
            if not (e == eng and v == self.cnt[e]):
                pass
            waits[e] = max(waits.get(e, 0), v)
        self.ops[eng].append((sorted(waits.items()), fn, inc))
        self.cnt[eng] += inc
        ev = (eng, self.cnt[eng])
        if chain:
            self.last = ev
        return ev

    def attach(self, nc, block, sems):
        for ename in ENGINES:
            ops = self.ops[ename]
            total = self.cnt[ename]

            def mk(ename=ename, ops=ops, total=total):
                def body(eng):
                    own = sems[ename]
                    for (waits, fn, inc) in ops:
                        for (we, wv) in waits:
                            eng.wait_ge(sems[we], wv)
                        fn(eng).then_inc(own, inc)
                    if total > 0:
                        eng.wait_ge(own, total)
                return body
            getattr(block, ename)(mk())


def _mk_dma(sb):
    def dma(eng, out, in_, deps=None, chain=True, **kw):
        return sb.emit(eng,
                       lambda e, out=out, in_=in_, kw=kw: e.dma_start(out=out, in_=in_, **kw),
                       deps=deps, dma=True, chain=chain)
    return dma


def build_A1():
    nc = bass.Bass("TRN2", target_bir_lowering=False, debug=False)
    sb = SB()
    dma = _mk_dma(sb)
    E = sb.emit

    x_in = nc.dram_tensor("x_in", [2 * SEQ * 1024], F32, kind="ExternalInput").ap()
    winT = nc.dram_tensor("winT", [1024, 4240], F32, kind="ExternalInput").ap()
    woutT = nc.dram_tensor("woutT", [2048, 1024], F32, kind="ExternalInput").ap()
    gwin = nc.dram_tensor("gwin", [1024, 8], F32, kind="ExternalInput").ap()
    cw_in = nc.dram_tensor("cw_in", [128, 64], F32, kind="ExternalInput").ap()
    cb_in = nc.dram_tensor("cb_in", [128, 16], F32, kind="ExternalInput").ap()
    cwbc_in = nc.dram_tensor("cwbc_in", [64, 8], F32, kind="ExternalInput").ap()
    cbbc_in = nc.dram_tensor("cbbc_in", [64, 2], F32, kind="ExternalInput").ap()
    dtb_in = nc.dram_tensor("dtb_in", [128, 16], F32, kind="ExternalInput").ap()
    alog_in = nc.dram_tensor("alog_in", [128, 16], F32, kind="ExternalInput").ap()
    dpar_in = nc.dram_tensor("dpar_in", [128, 16], F32, kind="ExternalInput").ap()
    nw_in = nc.dram_tensor("nw_in", [128, 2048], F32, kind="ExternalInput").ap()
    tri_in = nc.dram_tensor("tri_in", [128, 128], F32, kind="ExternalInput").ap()
    identf_in = nc.dram_tensor("identf_in", [128, 128], F32, kind="ExternalInput").ap()
    identb_in = nc.dram_tensor("identb_in", [128, 128], BF16, kind="ExternalInput").ap()
    onesrow_in = nc.dram_tensor("onesrow_in", [1, 128], F32, kind="ExternalInput").ap()
    ind_in = nc.dram_tensor("ind_in", [16, 2048], F32, kind="ExternalInput").ap()
    consts_in = nc.dram_tensor("consts_in", [128, 4], F32, kind="ExternalInput").ap()

    xn_ext = nc.dram_tensor("xn_ext", [2 * SEQ * 1040], BF16, kind="ExternalOutput").ap()
    probsT = nc.dram_tensor("probsT", [2 * 8 * SEQ], F32, kind="ExternalOutput").ap()
    x1base = nc.dram_tensor("x1base", [2 * SEQ * 1024], F32, kind="ExternalOutput").ap()

    from contextlib import ExitStack
    with ExitStack() as ctx:
        _e = ctx.enter_context
        winz = _e(nc.sbuf_tensor("winz", [128, 8 * 2064], BF16))
        winx = _e(nc.sbuf_tensor("winx", [128, 8 * 2176], BF16))
        wout = _e(nc.sbuf_tensor("wout", [128, 16 * 1024], BF16))
        gw = _e(nc.sbuf_tensor("gw", [128, 64], BF16))
        cw = _e(nc.sbuf_tensor("cw", [128, 64], F32))
        cb = _e(nc.sbuf_tensor("cb", [128, 16], F32))
        cwbc = _e(nc.sbuf_tensor("cwbc", [64, 8], F32))
        cbbc = _e(nc.sbuf_tensor("cbbc", [64, 2], F32))
        dtb = _e(nc.sbuf_tensor("dtb", [128, 16], F32))
        alog = _e(nc.sbuf_tensor("alog", [128, 16], F32))
        negA = _e(nc.sbuf_tensor("negA", [128, 16], F32))
        dpar = _e(nc.sbuf_tensor("dpar", [128, 16], F32))
        nwrep = _e(nc.sbuf_tensor("nwrep", [128, 2048], BF16))
        tri = _e(nc.sbuf_tensor("tri", [128, 128], F32))
        identf = _e(nc.sbuf_tensor("identf", [128, 128], F32))
        identb = _e(nc.sbuf_tensor("identb", [128, 128], BF16))
        onesrow = _e(nc.sbuf_tensor("onesrow", [1, 128], F32))
        indt = _e(nc.sbuf_tensor("indt", [16, 2048], F32))
        consts = _e(nc.sbuf_tensor("consts", [128, 4], F32))
        xt = _e(nc.sbuf_tensor("xt", [128, 1024], F32))
        xnf = _e(nc.sbuf_tensor("xnf", [128, 1024], F32))
        xnbf = _e(nc.sbuf_tensor("xnbf", [128, 1024], BF16))
        xnT = _e(nc.sbuf_tensor("xnT", [128, 8 * 512], BF16))
        colbuf = _e(nc.sbuf_tensor("colbuf", [128, 8], F32))
        prb = _e(nc.sbuf_tensor("prb", [128, 8], F32))
        prbbf = _e(nc.sbuf_tensor("prbbf", [128, 8], BF16))
        prT = _e(nc.sbuf_tensor("prT", [8, 128], F32))
        sz = _e(nc.sbuf_tensor("sz", [128, 2048], F32))
        dt16 = _e(nc.sbuf_tensor("dt16", [128, 16], F32))
        t16 = _e(nc.sbuf_tensor("t16", [128, 48], F32))
        xcmc = _e(nc.sbuf_tensor("xcmc", [128, 16 * 512], BF16))
        bcm = _e(nc.sbuf_tensor("bcm", [64, 512], BF16))
        ccm = _e(nc.sbuf_tensor("ccm", [64, 512], BF16))
        wb = _e(nc.sbuf_tensor("wb", [128, 515], F32))
        halo = _e(nc.sbuf_tensor("halo", [128, 16 * 3], F32))
        halobc = _e(nc.sbuf_tensor("halobc", [64, 6], F32))
        acc1 = _e(nc.sbuf_tensor("acc1", [128, 512], F32))
        xtm = _e(nc.sbuf_tensor("xtm", [128, 2048], BF16))
        btm = _e(nc.sbuf_tensor("btm", [128, 64], BF16))
        sfull = _e(nc.sbuf_tensor("sfull", [128, 16], F32))
        negs = _e(nc.sbuf_tensor("negs", [128, 16], F32))
        sT = _e(nc.sbuf_tensor("sT", [16, 128], F32))
        serow = _e(nc.sbuf_tensor("serow", [1, 16], F32))
        serep = _e(nc.sbuf_tensor("serep", [128, 16], F32))
        gm = _e(nc.sbuf_tensor("gm", [128, 128], F32))
        Et = _e(nc.sbuf_tensor("Et", [128, 128], F32))
        E2 = _e(nc.sbuf_tensor("E2", [64, 128], F32))
        Mt = _e(nc.sbuf_tensor("Mt", [128, 128], BF16))
        ctld = _e(nc.sbuf_tensor("ctld", [64, 128], BF16))
        wdt = _e(nc.sbuf_tensor("wdt", [128, 1], F32))
        bsc = _e(nc.sbuf_tensor("bsc", [128, 64], BF16))
        srun = _e(nc.sbuf_tensor("srun", [64, 2048], F32))
        srunbf = _e(nc.sbuf_tensor("srunbf", [64, 2048], BF16))
        ysb = _e(nc.sbuf_tensor("ysb", [128, 2048], F32))
        ygt = _e(nc.sbuf_tensor("ygt", [128, 2048], F32))
        ynbf = _e(nc.sbuf_tensor("ynbf", [128, 2048], BF16))
        ynT = _e(nc.sbuf_tensor("ynT", [128, 2048], BF16))
        x1p = xnf
        ps_ip = _e(nc.psum_tensor("ps_ip", [128, 1024], F32))
        ps_tr = _e(nc.psum_tensor("ps_tr", [128, 2048], BF16))
        ps_sm = _e(nc.psum_tensor("ps_sm", [128, 512], F32))
        ps_srep = _e(nc.psum_tensor("ps_srep", [128, 512], F32))
        ps_y = _e(nc.psum_tensor("ps_y", [128, 512], F32))
        ps_sc = _e(nc.psum_tensor("ps_sc", [64, 512], F32))
        s_sync = _e(nc.semaphore("sem_sync"))
        s_gp = _e(nc.semaphore("sem_gpsimd"))
        s_sc = _e(nc.semaphore("sem_scalar"))
        s_ve = _e(nc.semaphore("sem_vector"))
        s_te = _e(nc.semaphore("sem_tensor"))
        block = _e(nc.Block())
        sems = {"sync": s_sync, "gpsimd": s_gp, "scalar": s_sc,
                "vector": s_ve, "tensor": s_te}

        eps_ln = consts[:, 0:1]
        one_c = consts[:, 1:2]
        eps_rms = consts[:, 2:3]

        for db in range(8):
            dma("gpsimd", winz[:, db * 2064:db * 2064 + 2048],
                winT[db * 128:(db + 1) * 128, 0:2048])
            dma("gpsimd", winz[:, db * 2064 + 2048:db * 2064 + 2064],
                winT[db * 128:(db + 1) * 128, 4224:4240])
            dma("gpsimd", winx[:, db * 2176:(db + 1) * 2176],
                winT[db * 128:(db + 1) * 128, 2048:4224])
        dma("gpsimd", wout[:].rearrange("p (db n) -> p db n", db=16),
            AP(woutT.tensor, 0, [[1024, 128], [128 * 1024, 16], [1, 1024]]))
        dma("gpsimd", gw[:].rearrange("p (db n) -> p db n", db=8),
            AP(gwin.tensor, 0, [[8, 128], [128 * 8, 8], [1, 8]]))
        dma("sync", cw[:], cw_in)
        dma("sync", cb[:], cb_in)
        dma("sync", cwbc[:], cwbc_in)
        dma("sync", cbbc[:], cbbc_in)
        dma("sync", dtb[:], dtb_in)
        dma("sync", alog[:], alog_in)
        dma("sync", dpar[:], dpar_in)
        dma("gpsimd", nwrep[:], nw_in)
        dma("sync", tri[:], tri_in)
        dma("sync", identf[:], identf_in)
        dma("sync", identb[:], identb_in)
        dma("sync", onesrow[:], onesrow_in)
        dma("sync", indt[:], ind_in)
        dma("sync", consts[:], consts_in)
        E("scalar", lambda e: e.activation(out=negA[:], in_=alog[:], func=AF.Exp))
        E("vector", lambda e: e.tensor_scalar(out=negA[:], in0=negA[:], scalar1=-1.0,
                                              scalar2=None, op0=ALU.mult))

        for b in range(2):
            E("vector", lambda e: e.memset(srun[:], 0.0))
            E("vector", lambda e: e.memset(srunbf[:], 0.0))
            E("vector", lambda e: e.memset(halo[:], 0.0))
            E("vector", lambda e: e.memset(halobc[:], 0.0))
            for scidx in range(8):
                for cc in range(4):
                    c = scidx * 4 + cc
                    dma("sync", xt[:], AP(x_in.tensor, (b * SEQ + c * 128) * 1024,
                                          [[1024, 128], [1, 1024]]))
                    xc = xt[:, 0:1024]
                    E("vector", lambda e, xc=xc: e.reduce_sum(out=colbuf[:, 0:1], in_=xc, axis=AX.X))
                    E("vector", lambda e: e.tensor_scalar(out=colbuf[:, 0:1], in0=colbuf[:, 0:1],
                                                          scalar1=1.0 / 1024, scalar2=None, op0=ALU.mult))
                    E("vector", lambda e, xc=xc: e.tensor_scalar(out=xnf[:], in0=xc,
                                                                 scalar1=colbuf[:, 0:1], scalar2=None,
                                                                 op0=ALU.subtract))
                    E("scalar", lambda e: e.activation(out=ygt[:, 0:1024], in_=xnf[:], func=AF.Square))
                    E("vector", lambda e: e.reduce_sum(out=colbuf[:, 1:2], in_=ygt[:, 0:1024], axis=AX.X))
                    E("vector", lambda e: e.tensor_scalar(out=colbuf[:, 1:2], in0=colbuf[:, 1:2],
                                                          scalar1=1.0 / 1024, scalar2=None, op0=ALU.mult))
                    E("scalar", lambda e: e.activation(out=colbuf[:, 2:3], in_=colbuf[:, 1:2],
                                                       func=AF.Sqrt, bias=eps_ln))
                    E("vector", lambda e: e.reciprocal(out=colbuf[:, 3:4], in_=colbuf[:, 2:3]))
                    E("vector", lambda e: e.tensor_scalar(out=xnf[:], in0=xnf[:],
                                                          scalar1=colbuf[:, 3:4], scalar2=None,
                                                          op0=ALU.mult))
                    E("scalar", lambda e: e.copy(out=xnbf[:], in_=xnf[:]))
                    dma("sync", AP(xn_ext.tensor, (b * SEQ + c * 128) * 1040,
                                   [[1040, 128], [1, 1024]]), xnbf[:])
                    for db in range(8):
                        E("tensor", lambda e, db=db: e.transpose(
                            out=ps_tr[:, db * 128:(db + 1) * 128],
                            in_=xnbf[:, db * 128:(db + 1) * 128], identity=identb[:]))
                    E("vector", lambda e, cc=cc: e.tensor_copy(
                        out=AP(xnT[:].tensor, cc * 128, [[8 * 512, 128], [512, 8], [1, 128]]),
                        in_=ps_tr[:, 0:1024].rearrange("p (db t) -> p db t", db=8)))
                    for db in range(8):
                        E("tensor", lambda e, db=db, cc=cc: e.matmul(
                            ps_sm[:, 0:8],
                            xnT[:, db * 512 + cc * 128: db * 512 + (cc + 1) * 128],
                            gw[:, db * 8:(db + 1) * 8],
                            start=(db == 0), stop=(db == 7)))
                    E("vector", lambda e: e.reduce_max(out=colbuf[:, 4:5], in_=ps_sm[:, 0:8],
                                                       axis=AX.X, negate=True))
                    E("scalar", lambda e: e.activation(out=prb[:], in_=ps_sm[:, 0:8],
                                                       func=AF.Exp, bias=colbuf[:, 4:5]))
                    E("vector", lambda e: e.reduce_sum(out=colbuf[:, 5:6], in_=prb[:], axis=AX.X))
                    E("vector", lambda e: e.reciprocal(out=colbuf[:, 6:7], in_=colbuf[:, 5:6]))
                    E("vector", lambda e: e.tensor_scalar(out=prb[:], in0=prb[:],
                                                          scalar1=colbuf[:, 6:7], scalar2=None,
                                                          op0=ALU.mult))
                    E("scalar", lambda e: e.copy(out=prbbf[:], in_=prb[:]))
                    dma("sync", AP(xn_ext.tensor, (b * SEQ + c * 128) * 1040 + 1024,
                                   [[1040, 128], [1, 8]]), prbbf[:])
                    E("tensor", lambda e: e.transpose(out=ps_sm[0:8, 16:144], in_=prb[:],
                                                      identity=identf[:]))
                    E("vector", lambda e: e.tensor_copy(out=prT[:], in_=ps_sm[0:8, 16:144]))
                    dma("sync", AP(probsT.tensor, b * 8 * SEQ + c * 128,
                                   [[SEQ, 8], [1, 128]]), prT[:])
                for g in range(16):
                    for db in range(8):
                        E("tensor", lambda e, g=g, db=db: e.matmul(
                            ps_sm[:, 0:512],
                            winx[:, db * 2176 + g * 128: db * 2176 + (g + 1) * 128],
                            xnT[:, db * 512:(db + 1) * 512],
                            start=(db == 0), stop=(db == 7)))
                    E("vector", lambda e, g=g: e.tensor_copy(out=wb[:, 0:3], in_=halo[:, g * 3:g * 3 + 3]))
                    E("vector", lambda e: e.tensor_copy(out=wb[:, 3:515], in_=ps_sm[:, 0:512]))
                    E("vector", lambda e, g=g: e.tensor_copy(out=halo[:, g * 3:g * 3 + 3], in_=wb[:, 512:515]))
                    E("vector", lambda e, g=g: e.tensor_scalar(out=acc1[:], in0=wb[:, 0:512],
                                                               scalar1=cw[:, g * 4:g * 4 + 1],
                                                               scalar2=None, op0=ALU.mult))
                    for k in (1, 2, 3):
                        E("vector", lambda e, g=g, k=k: e.scalar_tensor_tensor(
                            out=acc1[:], in0=wb[:, k:k + 512],
                            scalar=cw[:, g * 4 + k:g * 4 + k + 1],
                            in1=acc1[:], op0=ALU.mult, op1=ALU.add))
                    E("scalar", lambda e, g=g: e.activation(out=xcmc[:, g * 512:(g + 1) * 512],
                                                            in_=acc1[:], func=AF.Silu,
                                                            bias=cb[:, g:g + 1]))
                for half, outt in ((0, bcm), (1, ccm)):
                    for db in range(8):
                        E("tensor", lambda e, half=half, db=db: e.matmul(
                            ps_sm[0:64, 0:512],
                            winx[:, db * 2176 + 2048 + half * 64: db * 2176 + 2048 + (half + 1) * 64],
                            xnT[:, db * 512:(db + 1) * 512],
                            start=(db == 0), stop=(db == 7)))
                    E("vector", lambda e, half=half: e.tensor_copy(
                        out=wb[0:64, 0:3], in_=halobc[:, half * 3:half * 3 + 3]))
                    E("vector", lambda e: e.tensor_copy(out=wb[0:64, 3:515], in_=ps_sm[0:64, 0:512]))
                    E("vector", lambda e, half=half: e.tensor_copy(
                        out=halobc[:, half * 3:half * 3 + 3], in_=wb[0:64, 512:515]))
                    E("vector", lambda e, half=half: e.tensor_scalar(
                        out=acc1[0:64, :], in0=wb[0:64, 0:512],
                        scalar1=cwbc[:, half * 4:half * 4 + 1], scalar2=None, op0=ALU.mult))
                    for k in (1, 2, 3):
                        E("vector", lambda e, half=half, k=k: e.scalar_tensor_tensor(
                            out=acc1[0:64, :], in0=wb[0:64, k:k + 512],
                            scalar=cwbc[:, half * 4 + k:half * 4 + k + 1],
                            in1=acc1[0:64, :], op0=ALU.mult, op1=ALU.add))
                    E("scalar", lambda e, outt=outt, half=half: e.activation(
                        out=outt[:], in_=acc1[0:64, :], func=AF.Silu,
                        bias=cbbc[:, half:half + 1]))
                for cc in range(4):
                    c = scidx * 4 + cc
                    dma("sync", xt[:], AP(x_in.tensor, (b * SEQ + c * 128) * 1024,
                                          [[1024, 128], [1, 1024]]))
                    for zg in range(4):
                        for db in range(8):
                            E("tensor", lambda e, zg=zg, db=db, cc=cc: e.matmul(
                                ps_ip[:, (zg % 2) * 512:(zg % 2) * 512 + 512],
                                xnT[:, db * 512 + cc * 128: db * 512 + (cc + 1) * 128],
                                winz[:, db * 2064 + zg * 512: db * 2064 + (zg + 1) * 512],
                                start=(db == 0), stop=(db == 7)))
                        E("scalar", lambda e, zg=zg: e.activation(
                            out=sz[:, zg * 512:(zg + 1) * 512],
                            in_=ps_ip[:, (zg % 2) * 512:(zg % 2) * 512 + 512], func=AF.Silu))
                    for db in range(8):
                        E("tensor", lambda e, db=db, cc=cc: e.matmul(
                            ps_sm[:, 0:16],
                            xnT[:, db * 512 + cc * 128: db * 512 + (cc + 1) * 128],
                            winz[:, db * 2064 + 2048: db * 2064 + 2064],
                            start=(db == 0), stop=(db == 7)))
                    E("vector", lambda e: e.tensor_tensor(out=t16[:, 0:16], in0=ps_sm[:, 0:16],
                                                          in1=dtb[:], op=ALU.add))
                    E("scalar", lambda e: e.activation(out=t16[:, 16:32], in_=t16[:, 0:16], func=AF.Abs))
                    E("scalar", lambda e: e.activation(out=t16[:, 16:32], in_=t16[:, 16:32],
                                                       func=AF.Exp, scale=-1.0))
                    E("scalar", lambda e: e.activation(out=t16[:, 16:32], in_=t16[:, 16:32],
                                                       func=AF.Ln, bias=one_c))
                    E("vector", lambda e: e.tensor_scalar(out=t16[:, 32:48], in0=t16[:, 0:16],
                                                          scalar1=0.0, scalar2=None, op0=ALU.max))
                    E("vector", lambda e: e.tensor_tensor(out=dt16[:], in0=t16[:, 16:32],
                                                          in1=t16[:, 32:48], op=ALU.add))
                    E("vector", lambda e: e.tensor_tensor(out=t16[:, 0:16], in0=dt16[:],
                                                          in1=negA[:], op=ALU.mult))
                    E("tensor", lambda e: e.matmul(ps_sm[:, 16:32], tri[:], t16[:, 0:16],
                                                   start=True, stop=True))
                    E("vector", lambda e: e.tensor_copy(out=sfull[:], in_=ps_sm[:, 16:32]))
                    E("vector", lambda e: e.tensor_scalar(out=negs[:], in0=sfull[:],
                                                          scalar1=-1.0, scalar2=None, op0=ALU.mult))
                    E("tensor", lambda e: e.transpose(out=ps_sm[0:16, 32:160], in_=sfull[:],
                                                      identity=identf[:]))
                    E("vector", lambda e: e.tensor_copy(out=sT[:], in_=ps_sm[0:16, 32:160]))
                    E("tensor", lambda e: e.transpose(out=ps_sm[0:1, 160:176], in_=sT[:, 127:128],
                                                      identity=identf[0:16, 0:16]))
                    E("vector", lambda e: e.tensor_copy(out=serow[:], in_=ps_sm[0:1, 160:176]))
                    E("tensor", lambda e: e.matmul(ps_sm[:, 176:192], onesrow[:], serow[:],
                                                   start=True, stop=True))
                    E("scalar", lambda e: e.activation(out=serep[:], in_=ps_sm[:, 176:192],
                                                       func=AF.Exp))
                    for hb in range(16):
                        E("tensor", lambda e, hb=hb, cc=cc: e.transpose(
                            out=ps_tr[:, hb * 128:(hb + 1) * 128],
                            in_=xcmc[:, hb * 512 + cc * 128: hb * 512 + (cc + 1) * 128],
                            identity=identb[:]))
                    E("vector", lambda e: e.tensor_copy(out=xtm[:], in_=ps_tr[:]))
                    E("tensor", lambda e, cc=cc: e.transpose(
                        out=ps_tr[:, 0:64], in_=bcm[:, cc * 128:(cc + 1) * 128],
                        identity=identb[0:64, 0:64]))
                    E("vector", lambda e: e.tensor_copy(out=btm[:], in_=ps_tr[:, 0:64]))
                    E("tensor", lambda e, cc=cc: e.matmul(
                        ps_sm[:, 192:320], bcm[:, cc * 128:(cc + 1) * 128],
                        ccm[:, cc * 128:(cc + 1) * 128], start=True, stop=True))
                    E("vector", lambda e: e.tensor_tensor(out=gm[:], in0=ps_sm[:, 192:320],
                                                          in1=tri[:], op=ALU.mult))
                    for h in range(16):
                        hq = h % 4
                        E("tensor", lambda e, h=h, hq=hq: e.matmul(
                            ps_srep[:, hq * 128:(hq + 1) * 128],
                            indt[:, h * 128:(h + 1) * 128], sT[:],
                            start=True, stop=True))
                        E("vector", lambda e, h=h, hq=hq: e.tensor_scalar(
                            out=Et[:], in0=ps_srep[:, hq * 128:(hq + 1) * 128],
                            scalar1=negs[:, h:h + 1], scalar2=0.0,
                            op0=ALU.add, op1=ALU.min))
                        E("scalar", lambda e: e.activation(out=Et[:], in_=Et[:], func=AF.Exp))
                        E("vector", lambda e, h=h: e.scalar_tensor_tensor(
                            out=Mt[:], in0=Et[:], scalar=dt16[:, h:h + 1], in1=gm[:],
                            op0=ALU.mult, op1=ALU.mult))
                        E("tensor", lambda e, h=h, hq=hq: e.matmul(
                            ps_y[:, hq * 128:(hq + 1) * 128], Mt[:],
                            xtm[:, h * 128:(h + 1) * 128], start=True, stop=False))
                        E("scalar", lambda e, hq=hq: e.activation(
                            out=E2[:], in_=ps_srep[0:64, hq * 128:hq * 128 + 128], func=AF.Exp))
                        E("vector", lambda e, cc=cc: e.tensor_tensor(
                            out=ctld[:], in0=ccm[:, cc * 128:(cc + 1) * 128], in1=E2[:],
                            op=ALU.mult))
                        E("tensor", lambda e, h=h, hq=hq: e.matmul(
                            ps_y[:, hq * 128:(hq + 1) * 128], ctld[:],
                            srunbf[:, h * 128:(h + 1) * 128], start=False, stop=True))
                        E("vector", lambda e, h=h: e.tensor_tensor(
                            out=wdt[:], in0=Et[:, 127:128], in1=dt16[:, h:h + 1], op=ALU.mult))
                        E("vector", lambda e: e.tensor_scalar(out=bsc[:], in0=btm[:],
                                                              scalar1=wdt[:, 0:1], scalar2=None,
                                                              op0=ALU.mult))
                        E("tensor", lambda e, h=h, hq=hq: e.matmul(
                            ps_sc[:, hq * 128:(hq + 1) * 128], bsc[:],
                            xtm[:, h * 128:(h + 1) * 128], start=True, stop=True))
                        E("vector", lambda e, h=h, hq=hq: e.scalar_tensor_tensor(
                            out=ysb[:, h * 128:(h + 1) * 128],
                            in0=xtm[:, h * 128:(h + 1) * 128],
                            scalar=dpar[:, h:h + 1],
                            in1=ps_y[:, hq * 128:(hq + 1) * 128],
                            op0=ALU.mult, op1=ALU.add))
                        E("vector", lambda e, h=h, hq=hq: e.scalar_tensor_tensor(
                            out=srun[:, h * 128:(h + 1) * 128],
                            in0=srun[:, h * 128:(h + 1) * 128],
                            scalar=serep[0:64, h:h + 1],
                            in1=ps_sc[:, hq * 128:(hq + 1) * 128],
                            op0=ALU.mult, op1=ALU.add))
                        E("scalar", lambda e, h=h: e.copy(
                            out=srunbf[:, h * 128:(h + 1) * 128],
                            in_=srun[:, h * 128:(h + 1) * 128]))
                    E("vector", lambda e: e.tensor_tensor(out=ygt[:], in0=ysb[:], in1=sz[:],
                                                          op=ALU.mult))
                    E("scalar", lambda e: e.activation(out=ysb[:], in_=ygt[:], func=AF.Square))
                    E("vector", lambda e: e.reduce_sum(out=colbuf[:, 0:1], in_=ysb[:], axis=AX.X))
                    E("vector", lambda e: e.tensor_scalar(out=colbuf[:, 0:1], in0=colbuf[:, 0:1],
                                                          scalar1=1.0 / 2048, scalar2=None,
                                                          op0=ALU.mult))
                    E("scalar", lambda e: e.activation(out=colbuf[:, 1:2], in_=colbuf[:, 0:1],
                                                       func=AF.Sqrt, bias=eps_rms))
                    E("vector", lambda e: e.reciprocal(out=colbuf[:, 2:3], in_=colbuf[:, 1:2]))
                    E("vector", lambda e: e.tensor_scalar(out=ygt[:], in0=ygt[:],
                                                          scalar1=colbuf[:, 2:3], scalar2=None,
                                                          op0=ALU.mult))
                    E("vector", lambda e: e.tensor_tensor(out=ynbf[:], in0=ygt[:], in1=nwrep[:],
                                                          op=ALU.mult))
                    for hb in range(16):
                        E("tensor", lambda e, hb=hb: e.transpose(
                            out=ps_tr[:, hb * 128:(hb + 1) * 128],
                            in_=ynbf[:, hb * 128:(hb + 1) * 128], identity=identb[:]))
                    E("vector", lambda e: e.tensor_copy(out=ynT[:], in_=ps_tr[:]))
                    for og in range(2):
                        for hb in range(16):
                            E("tensor", lambda e, og=og, hb=hb: e.matmul(
                                ps_ip[:, og * 512:(og + 1) * 512],
                                ynT[:, hb * 128:(hb + 1) * 128],
                                wout[:, hb * 1024 + og * 512: hb * 1024 + (og + 1) * 512],
                                start=(hb == 0), stop=(hb == 15)))
                        E("vector", lambda e, og=og: e.tensor_tensor(
                            out=x1p[:, og * 512:(og + 1) * 512],
                            in0=ps_ip[:, og * 512:(og + 1) * 512],
                            in1=xt[:, og * 512:(og + 1) * 512],
                            op=ALU.add))
                    dma("sync", AP(x1base.tensor, (b * SEQ + c * 128) * 1024,
                                   [[1024, 128], [1, 1024]]), x1p[:])

        sb.attach(nc, block, sems)
    return nc


def a1_host_inputs(core, inputs):
    ml = np
    import ml_dtypes
    b0 = core * 2
    u = inputs["L0_attn_u"]
    win = inputs["L0_attn_in_proj_w"]
    wo = inputs["L0_attn_out_proj_w"]
    cwf = inputs["L0_attn_conv_w"].reshape(2176, 4)
    cb = inputs["L0_attn_conv_b"]
    tri = np.ascontiguousarray(np.triu(np.ones((128, 128), np.float32)))
    ind = np.zeros((16, 2048), np.float32)
    for h in range(16):
        ind[h, h * 128:(h + 1) * 128] = 1.0
    consts = np.zeros((128, 4), np.float32)
    consts[:, 0] = LNEPS
    consts[:, 1] = 1.0
    consts[:, 2] = RMSEPS
    cw_x = cwf[:2048].reshape(16, 128, 4).transpose(1, 0, 2).reshape(128, 64)
    cb_x = cb[:2048].reshape(16, 128).T
    cw_bc = cwf[2048:].reshape(2, 64, 4).transpose(1, 0, 2).reshape(64, 8)
    cb_bc = cb[2048:].reshape(2, 64).T
    rep = lambda v: np.ascontiguousarray(np.broadcast_to(np.asarray(v, np.float32)[None, :], (128, len(v))))
    return {
        "x_in": np.ascontiguousarray(u[b0:b0 + 2]).reshape(-1),
        "winT": np.ascontiguousarray(win.T),
        "woutT": np.ascontiguousarray(wo.T),
        "gwin": np.ascontiguousarray(inputs["L0_ffn_gate_w"]),
        "cw_in": np.ascontiguousarray(cw_x),
        "cb_in": np.ascontiguousarray(cb_x),
        "cwbc_in": np.ascontiguousarray(cw_bc),
        "cbbc_in": np.ascontiguousarray(cb_bc),
        "dtb_in": rep(inputs["L0_attn_dt_bias"]),
        "alog_in": rep(inputs["L0_attn_A_log"]),
        "dpar_in": rep(inputs["L0_attn_D_param"]),
        "nw_in": np.ascontiguousarray(np.broadcast_to(inputs["L0_attn_norm_w"][None, :], (128, 2048))).astype(np.float32),
        "tri_in": tri,
        "identf_in": np.eye(128, dtype=np.float32),
        "identb_in": np.eye(128, dtype=np.float32).astype(ml_dtypes.bfloat16),
        "onesrow_in": np.ones((1, 128), np.float32),
        "ind_in": ind,
        "consts_in": consts,
    }


def build_A2():
    nc = bass.Bass("TRN2", target_bir_lowering=False, debug=False)
    sb = SB()
    dma = _mk_dma(sb)
    E = sb.emit

    xn_ext = nc.dram_tensor("xn_ext", [2 * SEQ * 1040], BF16, kind="ExternalInput").ap()
    probsT_in = nc.dram_tensor("probsT", [2 * 8 * SEQ], F32, kind="ExternalInput").ap()
    x1base = nc.dram_tensor("x1base", [2 * SEQ * 1024], F32, kind="ExternalInput").ap()
    w1_in = nc.dram_tensor("w1_in", [8, 1024, 4096], F32, kind="ExternalInput").ap()
    w2_in = nc.dram_tensor("w2_in", [8, 4096, 1024], F32, kind="ExternalInput").ap()
    identf_in = nc.dram_tensor("identf_in", [128, 128], F32, kind="ExternalInput").ap()
    indrep_in = nc.dram_tensor("indrep_in", [16, 128], F32, kind="ExternalInput").ap()
    indgrp_in = nc.dram_tensor("indgrp_in", [128, 16], F32, kind="ExternalInput").ap()
    slg_in = nc.dram_tensor("slg_in", [128, 128], F32, kind="ExternalInput").ap()
    srow_in = nc.dram_tensor("srow_in", [128, 1024], F32, kind="ExternalInput").ap()
    tid4_in = nc.dram_tensor("tid4_in", [128, 4096], F32, kind="ExternalInput").ap()
    tp_in = nc.dram_tensor("tp_in", [128, 96], BF16, kind="ExternalInput").ap()

    x1 = nc.dram_tensor("x1", [2 * SEQ * 1024], F32, kind="ExternalOutput").ap()
    ydram = nc.dram_tensor("ydram", [16 * 1024 * 1024], BF16, kind="Internal").ap()

    from contextlib import ExitStack
    with ExitStack() as ctx:
        _e = ctx.enter_context
        arena = _e(nc.sbuf_tensor("arena", [128, 40960], BF16))
        PT_all = arena[:, 0:32768]
        hT = arena[:, 0:16384]
        ysl = arena[:, 16384:24576]
        xinT = arena[:, 32768:40960]
        xacc = arena[:, 0:16384].bitcast(F32)
        ye = arena[:, 16384:24576]
        P2 = arena[:, 24576:32768]
        tid4 = _e(nc.sbuf_tensor("tid4", [128, 4096], F32))
        srow = _e(nc.sbuf_tensor("srow", [128, 1024], F32))
        tpall = _e(nc.sbuf_tensor("tpall", [128, 96], BF16))
        probs_sb = _e(nc.sbuf_tensor("probs_sb", [128, 512], F32))
        cmpt = _e(nc.sbuf_tensor("cmpt", [128, 512], F32))
        eqm = _e(nc.sbuf_tensor("eqm", [128, 512], F32))
        gtm = _e(nc.sbuf_tensor("gtm", [128, 512], F32))
        rnk = _e(nc.sbuf_tensor("rnk", [128, 512], F32))
        selm = _e(nc.sbuf_tensor("selm", [128, 512], F32))
        slotv = _e(nc.sbuf_tensor("slotv", [128, 512], F32))
        zt = _e(nc.sbuf_tensor("zt", [128, 512], F32))
        tT = _e(nc.sbuf_tensor("tT", [128, 512], F32))
        ipv = _e(nc.sbuf_tensor("ipv", [3, 1024], F32))
        ipvT = _e(nc.sbuf_tensor("ipvT", [128, 16 * 24], F32))
        idxc = _e(nc.sbuf_tensor("idxc", [128, 16 * 8], F32))
        pcb = _e(nc.sbuf_tensor("pcb", [128, 32], F32))
        prow32 = _e(nc.sbuf_tensor("prow32", [32, 128], F32))
        xnc = _e(nc.sbuf_tensor("xnc", [128, 2 * 1024], BF16))
        w1t = _e(nc.sbuf_tensor("w1t", [128, 2 * 1024], BF16))
        w2t = _e(nc.sbuf_tensor("w2t", [128, 2 * 1024], BF16))
        identf = _e(nc.sbuf_tensor("identf", [128, 128], F32))
        indrep = _e(nc.sbuf_tensor("indrep", [16, 128], F32))
        indgrp = _e(nc.sbuf_tensor("indgrp", [128, 16], F32))
        slg = _e(nc.sbuf_tensor("slg", [128, 128], F32))
        lo16 = _e(nc.sbuf_tensor("lo16", [16, 1], F32))
        hi16 = _e(nc.sbuf_tensor("hi16", [16, 1], F32))
        mid16 = _e(nc.sbuf_tensor("mid16", [16, 1], F32))
        cnt16 = _e(nc.sbuf_tensor("cnt16", [16, 1], F32))
        msk16 = _e(nc.sbuf_tensor("msk16", [16, 1], I32))
        mid128 = _e(nc.sbuf_tensor("mid128", [128, 1], F32))
        lo128 = _e(nc.sbuf_tensor("lo128", [128, 1], F32))
        hi128 = _e(nc.sbuf_tensor("hi128", [128, 1], F32))
        def128 = _e(nc.sbuf_tensor("def128", [128, 1], F32))
        cnt128 = _e(nc.sbuf_tensor("cnt128", [128, 1], F32))
        def16 = _e(nc.sbuf_tensor("def16", [16, 1], F32))
        carry = _e(nc.sbuf_tensor("carry", [128, 1], F32))
        segtot = _e(nc.sbuf_tensor("segtot", [128, 1], F32))
        pall = _e(nc.psum_tensor("pall", [128, 4096], F32))
        s_sync = _e(nc.semaphore("sem_sync"))
        s_gp = _e(nc.semaphore("sem_gpsimd"))
        s_sc = _e(nc.semaphore("sem_scalar"))
        s_ve = _e(nc.semaphore("sem_vector"))
        s_te = _e(nc.semaphore("sem_tensor"))
        block = _e(nc.Block())
        sems = {"sync": s_sync, "gpsimd": s_gp, "scalar": s_sc,
                "vector": s_ve, "tensor": s_te}

        dma("sync", probs_sb[:], AP(probsT_in.tensor, 0, [[512, 128], [1, 512]]))
        dma("sync", identf[:], identf_in)
        dma("sync", indrep[:], indrep_in)
        dma("sync", indgrp[:], indgrp_in)
        dma("sync", slg[:], slg_in)
        dma("sync", srow[:], srow_in)
        dma("sync", tid4[:], tid4_in)
        dma("sync", tpall[:], tp_in)
        E("vector", lambda e: e.memset(zt[:], 0.0))

        E("vector", lambda e: e.memset(lo16[:], 0.0))
        E("vector", lambda e: e.memset(hi16[:], 2.0))
        for it in range(64):
            E("vector", lambda e: e.tensor_tensor(out=mid16[:], in0=lo16[:], in1=hi16[:], op=ALU.add))
            E("vector", lambda e: e.tensor_scalar(out=mid16[:], in0=mid16[:], scalar1=0.5,
                                                  scalar2=None, op0=ALU.mult))
            E("tensor", lambda e: e.matmul(pall[:, 0:1], indrep[:], mid16[:], start=True, stop=True))
            E("vector", lambda e: e.tensor_copy(out=mid128[:], in_=pall[:, 0:1]))
            E("vector", lambda e: e.tensor_scalar(out=cmpt[:], in0=probs_sb[:], scalar1=mid128[:, 0:1],
                                                  scalar2=None, op0=ALU.is_ge))
            E("vector", lambda e: e.reduce_sum(out=cnt128[:], in_=cmpt[:], axis=AX.X))
            E("tensor", lambda e: e.matmul(pall[0:16, 8:9], indgrp[:], cnt128[:], start=True, stop=True))
            E("vector", lambda e: e.tensor_copy(out=cnt16[:], in_=pall[0:16, 8:9]))
            E("vector", lambda e: e.tensor_scalar(out=msk16[:], in0=cnt16[:], scalar1=1024.0,
                                                  scalar2=None, op0=ALU.is_ge))
            E("vector", lambda e: e.select(out=lo16[:], mask=msk16[:], on_true=mid16[:], on_false=lo16[:]))
            E("vector", lambda e: e.select(out=hi16[:], mask=msk16[:], on_true=hi16[:], on_false=mid16[:]))
        E("tensor", lambda e: e.matmul(pall[:, 0:1], indrep[:], lo16[:], start=True, stop=True))
        E("vector", lambda e: e.tensor_copy(out=lo128[:], in_=pall[:, 0:1]))
        E("tensor", lambda e: e.matmul(pall[:, 0:1], indrep[:], hi16[:], start=True, stop=True))
        E("vector", lambda e: e.tensor_copy(out=hi128[:], in_=pall[:, 0:1]))
        E("vector", lambda e: e.tensor_scalar(out=gtm[:], in0=probs_sb[:], scalar1=hi128[:, 0:1],
                                              scalar2=None, op0=ALU.is_ge))
        E("vector", lambda e: e.tensor_scalar(out=eqm[:], in0=probs_sb[:], scalar1=lo128[:, 0:1],
                                              scalar2=None, op0=ALU.is_ge))
        E("vector", lambda e: e.tensor_tensor(out=eqm[:], in0=eqm[:], in1=gtm[:], op=ALU.subtract))
        E("vector", lambda e: e.reduce_sum(out=cnt128[:], in_=gtm[:], axis=AX.X))
        E("tensor", lambda e: e.matmul(pall[0:16, 8:9], indgrp[:], cnt128[:], start=True, stop=True))
        E("vector", lambda e: e.tensor_copy(out=cnt16[:], in_=pall[0:16, 8:9]))
        E("vector", lambda e: e.tensor_scalar(out=def16[:], in0=cnt16[:], scalar1=-1.0,
                                              scalar2=1024.0, op0=ALU.mult, op1=ALU.add))
        E("tensor", lambda e: e.matmul(pall[:, 0:1], indrep[:], def16[:], start=True, stop=True))
        E("vector", lambda e: e.tensor_copy(out=def128[:], in_=pall[:, 0:1]))
        E("vector", lambda e: e.tensor_tensor_scan(out=rnk[:], data0=eqm[:], data1=zt[:],
                                                   initial=0.0, op0=ALU.add, op1=ALU.add))
        E("vector", lambda e: e.tensor_copy(out=segtot[:], in_=rnk[:, 511:512]))
        E("tensor", lambda e: e.matmul(pall[:, 0:1], slg[:], segtot[:], start=True, stop=True))
        E("vector", lambda e: e.tensor_copy(out=carry[:], in_=pall[:, 0:1]))
        E("vector", lambda e: e.tensor_scalar(out=rnk[:], in0=rnk[:], scalar1=carry[:, 0:1],
                                              scalar2=None, op0=ALU.add))
        E("vector", lambda e: e.tensor_scalar(out=cmpt[:], in0=rnk[:], scalar1=def128[:, 0:1],
                                              scalar2=None, op0=ALU.is_le))
        E("vector", lambda e: e.tensor_tensor(out=cmpt[:], in0=cmpt[:], in1=eqm[:], op=ALU.mult))
        E("vector", lambda e: e.tensor_tensor(out=selm[:], in0=gtm[:], in1=cmpt[:], op=ALU.add))
        E("vector", lambda e: e.tensor_tensor_scan(out=slotv[:], data0=selm[:], data1=zt[:],
                                                   initial=0.0, op0=ALU.add, op1=ALU.add))
        E("vector", lambda e: e.tensor_copy(out=segtot[:], in_=slotv[:, 511:512]))
        E("tensor", lambda e: e.matmul(pall[:, 0:1], slg[:], segtot[:], start=True, stop=True))
        E("vector", lambda e: e.tensor_copy(out=carry[:], in_=pall[:, 0:1]))
        E("vector", lambda e: e.tensor_scalar(out=slotv[:], in0=slotv[:], scalar1=carry[:, 0:1],
                                              scalar2=-1.0, op0=ALU.add, op1=ALU.add))
        E("vector", lambda e: e.tensor_scalar(out=slotv[:], in0=slotv[:], scalar1=-20000.0,
                                              scalar2=None, op0=ALU.add))
        E("vector", lambda e: e.tensor_tensor(out=slotv[:], in0=slotv[:], in1=selm[:], op=ALU.mult))
        E("vector", lambda e: e.tensor_scalar(out=slotv[:], in0=slotv[:], scalar1=20000.0,
                                              scalar2=None, op0=ALU.add))
        for j in range(4):
            E("tensor", lambda e, j=j: e.transpose(out=pall[:, 1024 + j * 128:1024 + (j + 1) * 128],
                                                   in_=slotv[:, j * 128:(j + 1) * 128],
                                                   identity=identf[:]))
        E("vector", lambda e: e.tensor_copy(out=tT[:], in_=pall[:, 1024:1536]))

        for ee in range(8):
            for b in range(2):
                be = b * 8 + ee
                dma("sync", prow32[:], AP(probsT_in.tensor, be * SEQ, [[128, 32], [1, 128]]))
                E("tensor", lambda e: e.transpose(out=pall[0:128, 1536:1568], in_=prow32[:],
                                                  identity=identf[0:32, 0:32]))
                E("vector", lambda e: e.tensor_copy(out=pcb[:], in_=pall[:, 1536:1568]))
                E("vector", lambda e: e.tensor_copy(
                    out=AP(tpall[:].tensor, 2, [[96, 128], [3, 32]]), in_=pcb[:]))
                for tc in range(32):
                    col = (tc % 4) * 128 + be * 8 + tc // 4
                    E("vector", lambda e, tc=tc, col=col: e.tensor_scalar(
                        out=PT_all[:, tc * 1024:(tc + 1) * 1024], in0=srow[:],
                        scalar1=tT[:, col:col + 1], scalar2=None, op0=ALU.is_equal))
                for sh in range(2):
                    for tc in range(32):
                        E("tensor", lambda e, sh=sh, tc=tc: e.matmul(
                            pall[0:3, sh * 512:(sh + 1) * 512],
                            tpall[:, tc * 3:(tc + 1) * 3],
                            PT_all[:, tc * 1024 + sh * 512: tc * 1024 + (sh + 1) * 512],
                            start=(tc == 0), stop=(tc == 31)))
                E("vector", lambda e: e.tensor_copy(out=ipv[:], in_=pall[0:3, 0:1024]))
                for sg in range(8):
                    E("tensor", lambda e, sg=sg: e.transpose(
                        out=pall[:, 1024 + sg * 128:1024 + sg * 128 + 3],
                        in_=ipv[0:3, sg * 128:(sg + 1) * 128], identity=identf[0:3, 0:3]))
                    E("vector", lambda e, be=be, sg=sg: e.tensor_copy(
                        out=ipvT[:, be * 24 + sg * 3: be * 24 + (sg + 1) * 3],
                        in_=pall[:, 1024 + sg * 128:1024 + sg * 128 + 3]))
                    E("vector", lambda e, be=be, sg=sg: e.scalar_tensor_tensor(
                        out=idxc[:, be * 8 + sg:be * 8 + sg + 1],
                        in0=ipvT[:, be * 24 + sg * 3:be * 24 + sg * 3 + 1],
                        scalar=64.0,
                        in1=ipvT[:, be * 24 + sg * 3 + 1:be * 24 + sg * 3 + 2],
                        op0=ALU.mult, op1=ALU.add))
                for sh in range(2):
                    for tc in range(32):
                        dma("sync", xnc[:, (tc % 2) * 1024:(tc % 2) * 1024 + 1024],
                            AP(xn_ext.tensor, (b * SEQ + tc * 128) * 1040, [[1040, 128], [1, 1024]]))
                        for db in range(8):
                            E("tensor", lambda e, sh=sh, tc=tc, db=db: e.matmul(
                                pall[:, db * 512:(db + 1) * 512],
                                xnc[:, (tc % 2) * 1024 + db * 128:(tc % 2) * 1024 + (db + 1) * 128],
                                PT_all[:, tc * 1024 + sh * 512: tc * 1024 + (sh + 1) * 512],
                                start=(tc == 0), stop=(tc == 31)))
                    for db in range(8):
                        E("scalar", lambda e, sh=sh, db=db: e.copy(
                            out=xinT[:, db * 1024 + sh * 512: db * 1024 + (sh + 1) * 512],
                            in_=pall[:, db * 512:(db + 1) * 512]))
                for half in range(2):
                    for ht in range(32):
                        dma("gpsimd", w1t[:, (ht % 2) * 1024:(ht % 2) * 1024 + 1024]
                            .rearrange("p (db n) -> p db n", db=8),
                            AP(w1_in.tensor, ee * 1024 * 4096 + ht * 128,
                               [[4096, 128], [128 * 4096, 8], [1, 128]]))
                        for db in range(8):
                            E("tensor", lambda e, ht=ht, db=db, half=half: e.matmul(
                                pall[:, 0:512],
                                w1t[:, (ht % 2) * 1024 + db * 128:(ht % 2) * 1024 + (db + 1) * 128],
                                xinT[:, db * 1024 + half * 512: db * 1024 + (half + 1) * 512],
                                start=(db == 0), stop=(db == 7)))
                        E("scalar", lambda e, ht=ht: e.activation(
                            out=hT[:, ht * 512:(ht + 1) * 512], in_=pall[:, 0:512], func=AF.Gelu))
                    for ht in range(32):
                        dma("gpsimd", w2t[:, (ht % 2) * 1024:(ht % 2) * 1024 + 1024],
                            AP(w2_in.tensor, ee * 4096 * 1024 + ht * 128 * 1024,
                               [[1024, 128], [1, 1024]]))
                        for sg in range(4):
                            for ch in range(2):
                                E("tensor", lambda e, ht=ht, sg=sg, ch=ch: e.matmul(
                                    pall[:, (sg * 2 + ch) * 512:(sg * 2 + ch + 1) * 512],
                                    hT[:, ht * 512 + sg * 128: ht * 512 + (sg + 1) * 128],
                                    w2t[:, (ht % 2) * 1024 + ch * 512:(ht % 2) * 1024 + (ch + 1) * 512],
                                    start=(ht == 0), stop=(ht == 31)))
                    for sg in range(4):
                        for ch in range(2):
                            E("vector", lambda e, sg=sg, ch=ch, half=half: e.tensor_copy(
                                out=ysl[:, (half * 4 + sg) * 1024 + ch * 512:
                                        (half * 4 + sg) * 1024 + (ch + 1) * 512],
                                in_=pall[:, (sg * 2 + ch) * 512:(sg * 2 + ch + 1) * 512]))
                dma("sync", AP(ydram.tensor, be * 1024 * 1024, [[1024, 128], [8192, 8], [1, 1024]])
                    .rearrange("p s n -> p (s n)") if False else
                    AP(ydram.tensor, be * 1024 * 1024, [[1024, 128], [128 * 1024, 8], [1, 1024]]),
                    ysl[:].rearrange("p (s n) -> p s n", s=8))

        for b in range(2):
            for tq in range(4):
                dma("sync", xacc[:].rearrange("p (c n) -> p c n", c=8),
                    AP(x1base.tensor, (b * SEQ + tq * 1024) * 1024,
                       [[1024, 128], [128 * 1024, 8], [1, 1024]]))
                for ee in range(8):
                    be = b * 8 + ee
                    dma("sync", ye[:].rearrange("p (s n) -> p s n", s=8),
                        AP(ydram.tensor, be * 1024 * 1024, [[1024, 128], [128 * 1024, 8], [1, 1024]]))
                    for sg in range(8):
                        E("vector", lambda e, be=be, sg=sg, tq=tq: e.tensor_scalar(
                            out=P2[:, sg * 1024:(sg + 1) * 1024],
                            in0=tid4[:, tq * 1024:(tq + 1) * 1024],
                            scalar1=idxc[:, be * 8 + sg:be * 8 + sg + 1],
                            scalar2=ipvT[:, be * 24 + sg * 3 + 2:be * 24 + sg * 3 + 3],
                            op0=ALU.is_equal, op1=ALU.mult))
                    for tcq in range(8):
                        for ch in range(2):
                            for sg in range(8):
                                E("tensor", lambda e, tcq=tcq, ch=ch, sg=sg: e.matmul(
                                    pall[:, 0:512],
                                    P2[:, sg * 1024 + tcq * 128: sg * 1024 + (tcq + 1) * 128],
                                    ye[:, sg * 1024 + ch * 512: sg * 1024 + (ch + 1) * 512],
                                    start=(sg == 0), stop=(sg == 7)))
                            E("vector", lambda e, tcq=tcq, ch=ch: e.tensor_tensor(
                                out=xacc[:, tcq * 1024 + ch * 512: tcq * 1024 + (ch + 1) * 512],
                                in0=pall[:, 0:512],
                                in1=xacc[:, tcq * 1024 + ch * 512: tcq * 1024 + (ch + 1) * 512],
                                op=ALU.add))
                dma("sync", AP(x1.tensor, (b * SEQ + tq * 1024) * 1024,
                               [[1024, 128], [128 * 1024, 8], [1, 1024]]),
                    xacc[:].rearrange("p (c n) -> p c n", c=8))

        sb.attach(nc, block, sems)
    return nc


def a2_host_inputs(core, inputs, prev):
    import ml_dtypes
    indrep = np.zeros((16, 128), np.float32)
    for k in range(16):
        indrep[k, k * 8:(k + 1) * 8] = 1.0
    slg = np.zeros((128, 128), np.float32)
    for q in range(128):
        for p in range(128):
            if q // 8 == p // 8 and q < p:
                slg[q, p] = 1.0
    srow = np.broadcast_to(np.arange(1024, dtype=np.float32)[None, :], (128, 1024))
    tid4 = np.broadcast_to(np.arange(4096, dtype=np.float32)[None, :], (128, 4096))
    tp = np.zeros((128, 96), np.float32)
    tl = np.arange(128)
    for tc in range(32):
        tp[:, tc * 3 + 0] = (tc * 128 + tl) // 64
        tp[:, tc * 3 + 1] = tl % 64
    return {
        "xn_ext": prev["xn_ext"],
        "probsT": prev["probsT"],
        "x1base": prev["x1base"],
        "w1_in": inputs["L0_ffn_w1"],
        "w2_in": inputs["L0_ffn_w2"],
        "identf_in": np.eye(128, dtype=np.float32),
        "indrep_in": indrep,
        "indgrp_in": np.ascontiguousarray(indrep.T),
        "slg_in": slg,
        "srow_in": np.ascontiguousarray(srow),
        "tid4_in": np.ascontiguousarray(tid4),
        "tp_in": tp.astype(ml_dtypes.bfloat16),
    }
def build_B():
    nc = bass.Bass("TRN2", target_bir_lowering=False, debug=False)
    sb = SB()
    dma = _mk_dma(sb)
    E = sb.emit

    x1_in = nc.dram_tensor("x1_in", [2 * SEQ * 1024], F32, kind="ExternalInput").ap()
    winT = nc.dram_tensor("winT", [1024, 3072], F32, kind="ExternalInput").ap()
    brep_in = nc.dram_tensor("brep_in", [128, 24], F32, kind="ExternalInput").ap()
    identb_in = nc.dram_tensor("identb_in", [128, 128], BF16, kind="ExternalInput").ap()
    consts_in = nc.dram_tensor("consts_in", [128, 4], F32, kind="ExternalInput").ap()

    xn1b = nc.dram_tensor("xn1b", [2 * SEQ * 1024], BF16, kind="ExternalOutput").ap()
    X = nc.dram_tensor("X", [3072 * 2 * SEQ], BF16, kind="ExternalOutput").ap()

    from contextlib import ExitStack
    with ExitStack() as ctx:
        _e = ctx.enter_context
        win = _e(nc.sbuf_tensor("win", [128, 8 * 3072], BF16))
        brep = _e(nc.sbuf_tensor("brep", [128, 24], F32))
        identb = _e(nc.sbuf_tensor("identb", [128, 128], BF16))
        consts = _e(nc.sbuf_tensor("consts", [128, 4], F32))
        xt = _e(nc.sbuf_tensor("xt", [128, 1024], F32))
        xnf = _e(nc.sbuf_tensor("xnf", [128, 1024], F32))
        sq = _e(nc.sbuf_tensor("sq", [128, 1024], F32))
        xnbf = _e(nc.sbuf_tensor("xnbf", [128, 1024], BF16))
        xnT = _e(nc.sbuf_tensor("xnT", [128, 8 * 512], BF16))
        colbuf = _e(nc.sbuf_tensor("colbuf", [128, 8], F32))
        ev = _e(nc.sbuf_tensor("ev", [128, 512], BF16))
        ps_tr = _e(nc.psum_tensor("ps_tr", [128, 1024], BF16))
        ps_m = _e(nc.psum_tensor("ps_m", [128, 512], F32))
        s_sync = _e(nc.semaphore("sem_sync"))
        s_gp = _e(nc.semaphore("sem_gpsimd"))
        s_sc = _e(nc.semaphore("sem_scalar"))
        s_ve = _e(nc.semaphore("sem_vector"))
        s_te = _e(nc.semaphore("sem_tensor"))
        block = _e(nc.Block())
        sems = {"sync": s_sync, "gpsimd": s_gp, "scalar": s_sc,
                "vector": s_ve, "tensor": s_te}

        for db in range(8):
            dma("gpsimd", win[:, db * 3072:(db + 1) * 3072],
                winT[db * 128:(db + 1) * 128, :])
        dma("sync", brep[:], brep_in)
        dma("sync", identb[:], identb_in)
        dma("sync", consts[:], consts_in)
        eps_ln = consts[:, 0:1]

        for b in range(2):
            for scidx in range(8):
                for cc in range(4):
                    c = scidx * 4 + cc
                    dma("sync", xt[:], AP(x1_in.tensor, (b * SEQ + c * 128) * 1024,
                                          [[1024, 128], [1, 1024]]))
                    E("vector", lambda e: e.reduce_sum(out=colbuf[:, 0:1], in_=xt[:], axis=AX.X))
                    E("vector", lambda e: e.tensor_scalar(out=colbuf[:, 0:1], in0=colbuf[:, 0:1],
                                                          scalar1=1.0 / 1024, scalar2=None, op0=ALU.mult))
                    E("vector", lambda e: e.tensor_scalar(out=xnf[:], in0=xt[:],
                                                          scalar1=colbuf[:, 0:1], scalar2=None,
                                                          op0=ALU.subtract))
                    E("scalar", lambda e: e.activation(out=sq[:], in_=xnf[:], func=AF.Square))
                    E("vector", lambda e: e.reduce_sum(out=colbuf[:, 1:2], in_=sq[:], axis=AX.X))
                    E("vector", lambda e: e.tensor_scalar(out=colbuf[:, 1:2], in0=colbuf[:, 1:2],
                                                          scalar1=1.0 / 1024, scalar2=None, op0=ALU.mult))
                    E("scalar", lambda e: e.activation(out=colbuf[:, 2:3], in_=colbuf[:, 1:2],
                                                       func=AF.Sqrt, bias=eps_ln))
                    E("vector", lambda e: e.reciprocal(out=colbuf[:, 3:4], in_=colbuf[:, 2:3]))
                    E("vector", lambda e: e.tensor_scalar(out=xnf[:], in0=xnf[:],
                                                          scalar1=colbuf[:, 3:4], scalar2=None,
                                                          op0=ALU.mult))
                    E("scalar", lambda e: e.copy(out=xnbf[:], in_=xnf[:]))
                    dma("sync", AP(xn1b.tensor, (b * SEQ + c * 128) * 1024,
                                   [[1024, 128], [1, 1024]]), xnbf[:])
                    for db in range(8):
                        E("tensor", lambda e, db=db: e.transpose(
                            out=ps_tr[:, db * 128:(db + 1) * 128],
                            in_=xnbf[:, db * 128:(db + 1) * 128], identity=identb[:]))
                    E("vector", lambda e, cc=cc: e.tensor_copy(
                        out=AP(xnT[:].tensor, cc * 128, [[8 * 512, 128], [512, 8], [1, 128]]),
                        in_=ps_tr[:, 0:1024].rearrange("p (db t) -> p db t", db=8)))
                for g in range(24):
                    for db in range(8):
                        E("tensor", lambda e, g=g, db=db: e.matmul(
                            ps_m[:],
                            win[:, db * 3072 + g * 128: db * 3072 + (g + 1) * 128],
                            xnT[:, db * 512:(db + 1) * 512], start=(db == 0), stop=(db == 7)))
                    E("scalar", lambda e, g=g: e.add(out=ev[:], in_=ps_m[:], add=brep[:, g:g + 1]))
                    dma("sync", AP(X.tensor, (g * 128) * 2 * SEQ + b * SEQ + scidx * 512,
                                   [[2 * SEQ, 128], [1, 512]]), ev[:])

        sb.attach(nc, block, sems)
    return nc


def b_host_inputs(core, inputs, x1):
    import ml_dtypes
    consts = np.zeros((128, 4), np.float32)
    consts[:, 0] = LNEPS
    b_in = inputs["L1_attn_b_in"]
    brep = b_in.reshape(24, 128).T
    return {
        "x1_in": x1,
        "winT": np.ascontiguousarray(inputs["L1_attn_w_in"].T),
        "brep_in": np.ascontiguousarray(brep).astype(np.float32),
        "identb_in": np.eye(128, dtype=np.float32).astype(ml_dtypes.bfloat16),
        "consts_in": consts,
    }
def build_C():
    nc = bass.Bass("TRN2", target_bir_lowering=False, debug=False)
    sb = SB()
    dma = _mk_dma(sb)
    E = sb.emit

    Xc = nc.dram_tensor("Xc", [3 * 128 * 16 * SEQ], BF16, kind="ExternalInput").ap()
    peT_in = nc.dram_tensor("peT_in", [33, SEQ], F32, kind="ExternalInput").ap()
    w1h_in = nc.dram_tensor("w1h_in", [33, 64], F32, kind="ExternalInput").ap()
    b1c_in = nc.dram_tensor("b1c_in", [64, 1], F32, kind="ExternalInput").ap()
    w2sel_in = nc.dram_tensor("w2sel_in", [64, 256], BF16, kind="ExternalInput").ap()
    b2c_in = nc.dram_tensor("b2c_in", [128, 2], F32, kind="ExternalInput").ap()
    fbc_in = nc.dram_tensor("fbc_in", [128, 2], F32, kind="ExternalInput").ap()
    fdc_in = nc.dram_tensor("fdc_in", [128, 2], F32, kind="ExternalInput").ap()
    iott_in = nc.dram_tensor("iott_in", [128, SEQ], F32, kind="ExternalInput").ap()
    jrev_in = nc.dram_tensor("jrev_in", [128, 128], BF16, kind="ExternalInput").ap()
    identb_in = nc.dram_tensor("identb_in", [128, 128], BF16, kind="ExternalInput").ap()
    cw3_in = nc.dram_tensor("cw3_in", [128, 9], F32, kind="ExternalInput").ap()
    cb3_in = nc.dram_tensor("cb3_in", [128, 3], F32, kind="ExternalInput").ap()

    Y = nc.dram_tensor("Y", [128 * 16 * SEQ], BF16, kind="ExternalOutput").ap()
    hp = nc.dram_tensor("hp", [256 * 4352], BF16, kind="Internal").ap()
    Xc2 = nc.dram_tensor("Xc2", [3 * 128 * 16 * SEQ], BF16, kind="Internal").ap()

    from contextlib import ExitStack
    with ExitStack() as ctx:
        _e = ctx.enter_context
        peT = _e(nc.sbuf_tensor("peT", [33, SEQ], F32))
        w1h = _e(nc.sbuf_tensor("w1h", [33, 64], F32))
        b1c = _e(nc.sbuf_tensor("b1c", [64, 1], F32))
        w2sel = _e(nc.sbuf_tensor("w2sel", [64, 256], BF16))
        b2c = _e(nc.sbuf_tensor("b2c", [128, 2], F32))
        fbc = _e(nc.sbuf_tensor("fbc", [128, 2], F32))
        fdc = _e(nc.sbuf_tensor("fdc", [128, 2], F32))
        mucol = _e(nc.sbuf_tensor("mucol", [128, 2], F32))
        iott = _e(nc.sbuf_tensor("iott", [128, SEQ], F32))
        jrev = _e(nc.sbuf_tensor("jrev", [128, 128], BF16))
        identb = _e(nc.sbuf_tensor("identb", [128, 128], BF16))
        cw3 = _e(nc.sbuf_tensor("cw3", [128, 9], F32))
        cb3 = _e(nc.sbuf_tensor("cb3", [128, 3], F32))
        sbb = _e(nc.sbuf_tensor("sbb", [64, SEQ], BF16))
        argt = _e(nc.sbuf_tensor("argt", [64, 512], F32))
        argi = _e(nc.sbuf_tensor("argi", [64, 512], I32))
        argf = _e(nc.sbuf_tensor("argf", [64, 512], F32))
        dect = _e(nc.sbuf_tensor("dect", [128, 512], F32))
        hpt1 = _e(nc.sbuf_tensor("hpt1", [128, 512], F32))
        hptb = _e(nc.sbuf_tensor("hptb", [128, 512], BF16))
        zpad = _e(nc.sbuf_tensor("zpad", [128, 129], BF16))
        xr = _e(nc.sbuf_tensor("xr", [128, SEQ], BF16))
        accw = _e(nc.sbuf_tensor("accw", [128, SEQ], F32))
        xcv = _e(nc.sbuf_tensor("xcv", [128, SEQ], BF16))
        hank = _e(nc.sbuf_tensor("hank", [128, 32 * 128], BF16))
        vst = _e(nc.sbuf_tensor("vst", [128, 512], BF16))
        vrev = _e(nc.sbuf_tensor("vrev", [128, 512], BF16))
        gt0 = _e(nc.sbuf_tensor("gt0", [128, 512], BF16))
        gt1 = _e(nc.sbuf_tensor("gt1", [128, 512], BF16))
        vout = _e(nc.sbuf_tensor("vout", [128, 512], BF16))
        ps_a = _e(nc.psum_tensor("ps_a", [128, 512], F32))
        ps_p = _e(nc.psum_tensor("ps_p", [128, 512], F32))
        ps_tr = _e(nc.psum_tensor("ps_tr", [128, 512], BF16))
        s_sync = _e(nc.semaphore("sem_sync"))
        s_gp = _e(nc.semaphore("sem_gpsimd"))
        s_sc = _e(nc.semaphore("sem_scalar"))
        s_ve = _e(nc.semaphore("sem_vector"))
        s_te = _e(nc.semaphore("sem_tensor"))
        block = _e(nc.Block())
        sems = {"sync": s_sync, "gpsimd": s_gp, "scalar": s_sc,
                "vector": s_ve, "tensor": s_te}
        twopi = float(2 * np.pi)

        dma("sync", peT[:], peT_in)
        dma("sync", w1h[:], w1h_in)
        dma("sync", b1c[:], b1c_in)
        dma("sync", w2sel[:], w2sel_in)
        dma("sync", b2c[:], b2c_in)
        dma("sync", fbc[:], fbc_in)
        dma("sync", fdc[:], fdc_in)
        dma("sync", iott[:], iott_in)
        dma("sync", jrev[:], jrev_in)
        dma("sync", identb[:], identb_in)
        dma("sync", cw3[:], cw3_in)
        dma("sync", cb3[:], cb3_in)
        E("scalar", lambda e: e.activation(out=mucol[:], in_=fdc[:], func=AF.Exp))
        E("vector", lambda e: e.memset(zpad[:], 0.0))

        for cs in range(8):
            E("tensor", lambda e, cs=cs: e.matmul(ps_a[0:64, 0:512], w1h[:],
                                                  peT[:, cs * 512:(cs + 1) * 512],
                                                  start=True, stop=True))
            E("vector", lambda e: e.tensor_scalar(out=argt[:], in0=ps_a[0:64, 0:512],
                                                  scalar1=b1c[:, 0:1], scalar2=10.0,
                                                  op0=ALU.add, op1=ALU.mult))
            E("vector", lambda e: e.tensor_scalar(out=argf[:], in0=argt[:],
                                                  scalar1=1.0 / twopi, scalar2=None, op0=ALU.mult))
            E("vector", lambda e: e.tensor_copy(out=argi[:], in_=argf[:]))
            E("vector", lambda e: e.tensor_copy(out=argf[:], in_=argi[:]))
            E("vector", lambda e: e.scalar_tensor_tensor(out=argt[:], in0=argf[:], scalar=-twopi,
                                                         in1=argt[:], op0=ALU.mult, op1=ALU.add))
            E("scalar", lambda e, cs=cs: e.activation(out=sbb[:, cs * 512:(cs + 1) * 512],
                                                      in_=argt[:], func=AF.Sin))
        for n in range(2):
            for cs in range(8):
                E("tensor", lambda e, n=n, cs=cs: e.matmul(
                    ps_p[:, 0:512], w2sel[:, n * 128:(n + 1) * 128],
                    sbb[:, cs * 512:(cs + 1) * 512], start=True, stop=True))
                E("vector", lambda e, n=n, cs=cs: e.tensor_scalar(
                    out=dect[:], in0=iott[:, cs * 512:(cs + 1) * 512],
                    scalar1=mucol[:, n:n + 1], scalar2=-1.0 / 4095.0,
                    op0=ALU.mult, op1=ALU.mult))
                E("scalar", lambda e: e.activation(out=dect[:], in_=dect[:], func=AF.Exp))
                E("scalar", lambda e, n=n: e.add(out=hpt1[:], in_=ps_p[:, 0:512], add=b2c[:, n:n + 1]))
                E("vector", lambda e: e.tensor_tensor(out=hpt1[:], in0=hpt1[:], in1=dect[:], op=ALU.mult))
                E("vector", lambda e, n=n: e.tensor_scalar(out=hptb[:], in0=hpt1[:],
                                                           scalar1=fbc[:, n:n + 1],
                                                           scalar2=1.0 / 4096.0,
                                                           op0=ALU.add, op1=ALU.mult))
                dma("sync", AP(hp.tensor, n * 128 * 4352 + 127 + cs * 512,
                               [[4352, 128], [1, 512]]), hptb[:])
            dma("sync", AP(hp.tensor, n * 128 * 4352, [[4352, 128], [1, 127]]), zpad[:, 0:127])
            dma("sync", AP(hp.tensor, n * 128 * 4352 + 127 + 4096, [[4352, 128], [1, 129]]), zpad[:])

        for g in range(3):
            for b in range(16):
                dma("sync", xr[:], AP(Xc.tensor, g * 128 * 16 * SEQ + b * SEQ,
                                      [[16 * SEQ, 128], [1, SEQ]]))
                E("vector", lambda e, g=g: e.tensor_scalar(out=accw[:], in0=xr[:],
                                                           scalar1=cw3[:, g * 3 + 1:g * 3 + 2],
                                                           scalar2=None, op0=ALU.mult))
                E("vector", lambda e, g=g: e.scalar_tensor_tensor(
                    out=accw[:, 1:SEQ], in0=xr[:, 0:SEQ - 1],
                    scalar=cw3[:, g * 3 + 0:g * 3 + 1],
                    in1=accw[:, 1:SEQ], op0=ALU.mult, op1=ALU.add))
                E("vector", lambda e, g=g: e.scalar_tensor_tensor(
                    out=accw[:, 0:SEQ - 1], in0=xr[:, 1:SEQ],
                    scalar=cw3[:, g * 3 + 2:g * 3 + 3],
                    in1=accw[:, 0:SEQ - 1], op0=ALU.mult, op1=ALU.add))
                E("vector", lambda e, g=g: e.tensor_scalar(out=xcv[:], in0=accw[:],
                                                           scalar1=cb3[:, g:g + 1],
                                                           scalar2=None, op0=ALU.add))
                dma("sync", AP(Xc2.tensor, g * 128 * 16 * SEQ + b * SEQ,
                               [[16 * SEQ, 128], [1, SEQ]]), xcv[:])

        for dl in range(128):
            dma("sync", vst[:], AP(Xc2.tensor, (2 * 128 + dl) * 16 * SEQ,
                                   [[128, 512], [1, 128]]), transpose=True)
            dma("sync", gt0[:], AP(Xc2.tensor, (0 * 128 + dl) * 16 * SEQ,
                                   [[128, 512], [1, 128]]), transpose=True)
            dma("sync", gt1[:], AP(Xc2.tensor, (1 * 128 + dl) * 16 * SEQ,
                                   [[128, 512], [1, 128]]), transpose=True)
            for stage in range(2):
                dma("sync", hank[:].rearrange("p (k a) -> p k a", k=32),
                    AP(hp.tensor, (stage * 128 + dl) * 4352, [[1, 128], [128, 32], [1, 128]]))
                src = vst if stage == 0 else vout
                E("tensor", lambda e, src=src: e.matmul(ps_a[:, 0:512], jrev[:], src[:],
                                                        start=True, stop=True))
                E("vector", lambda e: e.tensor_copy(out=vrev[:], in_=ps_a[:, 0:512]))
                for k in range(32):
                    E("tensor", lambda e, k=k: e.matmul(
                        AP(ps_p[:].tensor, k, [[512, 128], [32, 16], [1, 32 - k]]),
                        hank[:, k * 128:(k + 1) * 128],
                        AP(vrev[:].tensor, 0, [[512, 128], [32, 16], [1, 32 - k]]),
                        start=(k == 0), stop=(k == 31)))
                gate = gt0 if stage == 0 else gt1
                E("vector", lambda e, gate=gate: e.tensor_tensor(out=vout[:], in0=ps_p[:, 0:512],
                                                                 in1=gate[:], op=ALU.mult))
            for j in range(4):
                E("tensor", lambda e, j=j: e.transpose(
                    out=ps_tr[:, 0:128], in_=vout[:, j * 128:(j + 1) * 128],
                    identity=identb[:]))
                E("vector", lambda e: e.tensor_copy(out=vrev[:, 0:128], in_=ps_tr[:, 0:128]))
                dma("sync", AP(Y.tensor, dl * 16 * SEQ + j * 4 * SEQ,
                               [[SEQ, 4], [128, 32], [1, 128]]), vrev[:, 0:128])

        sb.attach(nc, block, sems)
    return nc


def c_host_inputs(core, inputs, X_full):
    import ml_dtypes
    d0 = core * 128
    rows = np.concatenate([X_full[n * 1024 + d0:n * 1024 + d0 + 128] for n in range(3)], axis=0)
    t = np.arange(SEQ, dtype=np.float64) / (SEQ - 1)
    bands = np.arange(33, dtype=np.float64)
    pe = np.sin(t[:, None] * bands[None, :] * 10.0)
    peT = np.ascontiguousarray(pe.T).astype(np.float32)
    w1 = inputs["L1_attn_w1"]
    w2 = inputs["L1_attn_w2"]
    w2sel = np.concatenate([w2[n * 1024 + d0:n * 1024 + d0 + 128] for n in range(2)], axis=0)
    b2 = inputs["L1_attn_b2"]
    b2c = np.stack([b2[n * 1024 + d0:n * 1024 + d0 + 128] for n in range(2)], axis=1)
    fb = inputs["L1_attn_filt_bias"]
    fbc = np.stack([fb[n, d0:d0 + 128] for n in range(2)], axis=1)
    fd = inputs["L1_attn_filt_decay"]
    fdc = np.stack([fd[n, d0:d0 + 128] for n in range(2)], axis=1)
    iott = np.broadcast_to(np.arange(SEQ, dtype=np.float32)[None, :], (128, SEQ))
    jrev = np.eye(128, dtype=np.float32)[::-1].copy()
    cw = inputs["L1_attn_conv_w"]
    cw3 = np.concatenate([cw[g * 1024 + d0:g * 1024 + d0 + 128] for g in range(3)], axis=1)
    cb = inputs["L1_attn_conv_b"]
    cb3 = np.stack([cb[g * 1024 + d0:g * 1024 + d0 + 128] for g in range(3)], axis=1)
    return {
        "Xc": np.ascontiguousarray(rows).reshape(-1),
        "peT_in": peT,
        "w1h_in": np.ascontiguousarray(w1.T),
        "b1c_in": inputs["L1_attn_b1"][:, None].astype(np.float32),
        "w2sel_in": np.ascontiguousarray(w2sel.T).astype(ml_dtypes.bfloat16),
        "b2c_in": np.ascontiguousarray(b2c).astype(np.float32),
        "fbc_in": np.ascontiguousarray(fbc).astype(np.float32),
        "fdc_in": np.ascontiguousarray(fdc).astype(np.float32),
        "iott_in": np.ascontiguousarray(iott),
        "jrev_in": jrev.astype(ml_dtypes.bfloat16),
        "identb_in": np.eye(128, dtype=np.float32).astype(ml_dtypes.bfloat16),
        "cw3_in": np.ascontiguousarray(cw3).astype(np.float32),
        "cb3_in": np.ascontiguousarray(cb3).astype(np.float32),
    }
def build_D():
    nc = bass.Bass("TRN2", target_bir_lowering=False, debug=False)
    sb = SB()
    dma = _mk_dma(sb)
    E = sb.emit

    x1_in = nc.dram_tensor("x1_in", [2 * SEQ * 1024], F32, kind="ExternalInput").ap()
    xn1b_in = nc.dram_tensor("xn1b_in", [2 * SEQ * 1024], BF16, kind="ExternalInput").ap()
    v2_in = nc.dram_tensor("v2_in", [2 * 1024 * SEQ], BF16, kind="ExternalInput").ap()
    sim_in = nc.dram_tensor("sim_in", [1024, 16], F32, kind="ExternalInput").ap()
    gates_in = nc.dram_tensor("gates_in", [128, 16], F32, kind="ExternalInput").ap()
    qkv_in = nc.dram_tensor("qkv_in", [16, 1024, 192], F32, kind="ExternalInput").ap()
    o_in = nc.dram_tensor("o_in", [16, 64, 1024], F32, kind="ExternalInput").ap()
    woT_in = nc.dram_tensor("woT_in", [1024, 1024], F32, kind="ExternalInput").ap()
    bout_in = nc.dram_tensor("bout_in", [128, 1024], F32, kind="ExternalInput").ap()
    identb_in = nc.dram_tensor("identb_in", [128, 128], BF16, kind="ExternalInput").ap()
    identf_in = nc.dram_tensor("identf_in", [128, 128], F32, kind="ExternalInput").ap()
    penal_in = nc.dram_tensor("penal_in", [128, 128], F32, kind="ExternalInput").ap()
    penalT_in = nc.dram_tensor("penalT_in", [128, 128], F32, kind="ExternalInput").ap()
    onescol_in = nc.dram_tensor("onescol_in", [128, 1], F32, kind="ExternalInput").ap()
    onesrow_in = nc.dram_tensor("onesrow_in", [1, 128], F32, kind="ExternalInput").ap()

    out_t = nc.dram_tensor("out_t", [2 * SEQ * 1024], F32, kind="ExternalOutput").ap()

    from contextlib import ExitStack
    with ExitStack() as ctx:
        _e = ctx.enter_context
        arena = _e(nc.sbuf_tensor("arena", [128, 32768], BF16))
        xn1T = arena[:, 0:32768]
        v2sb = arena[:, 0:32768]
        qkvacc = _e(nc.sbuf_tensor("qkvacc", [128, 32 * 192], F32))
        rwres = _e(nc.sbuf_tensor("rwres", [128, 2 * 32 * 16], F32))
        xnc = _e(nc.sbuf_tensor("xnc", [128, 1024], BF16))
        xncf = _e(nc.sbuf_tensor("xncf", [128, 1024], F32))
        x1t = _e(nc.sbuf_tensor("x1t", [128, 1024], F32))
        outt = _e(nc.sbuf_tensor("outt", [128, 1024], F32))
        colb = _e(nc.sbuf_tensor("colb", [128, 16], F32))
        lgt = _e(nc.sbuf_tensor("lgt", [128, 16], F32))
        lgt2 = _e(nc.sbuf_tensor("lgt2", [128, 16], F32))
        actm = _e(nc.sbuf_tensor("actm", [128, 16], F32))
        fbm = _e(nc.sbuf_tensor("fbm", [128, 16], F32))
        rwt = _e(nc.sbuf_tensor("rwt", [128, 16], F32))
        sim = _e(nc.sbuf_tensor("sim", [128, 128], F32))
        simn = _e(nc.sbuf_tensor("simn", [128, 128], BF16))
        sgrep = _e(nc.sbuf_tensor("sgrep", [128, 16], F32))
        normrep = _e(nc.sbuf_tensor("normrep", [128, 16], F32))
        nrow = _e(nc.sbuf_tensor("nrow", [1, 16], F32))
        qkvw = _e(nc.sbuf_tensor("qkvw", [128, 2 * 8 * 192], BF16))
        qT = _e(nc.sbuf_tensor("qT", [64, 32 * 128], BF16))
        kT = _e(nc.sbuf_tensor("kT", [64, 32 * 128], BF16))
        vbf = _e(nc.sbuf_tensor("vbf", [128, 32 * 65], BF16))
        qsc = _e(nc.sbuf_tensor("qsc", [128, 64], BF16))
        runm = _e(nc.sbuf_tensor("runm", [128, 1], F32))
        rmt = _e(nc.sbuf_tensor("rmt", [128, 1], F32))
        nmrep = _e(nc.sbuf_tensor("nmrep", [128, 128], F32))
        nmrow = _e(nc.sbuf_tensor("nmrow", [1, 128], F32))
        stf = _e(nc.sbuf_tensor("stf", [128, 128], F32))
        ptb = _e(nc.sbuf_tensor("ptb", [128, 128], BF16))
        rcol = _e(nc.sbuf_tensor("rcol", [128, 1], F32))
        attb = _e(nc.sbuf_tensor("attb", [128, 64], BF16))
        attT = _e(nc.sbuf_tensor("attT", [64, 128], BF16))
        oacc = _e(nc.sbuf_tensor("oacc", [128, 1024], F32))
        Ob = _e(nc.sbuf_tensor("Ob", [64, 16 * 1024], BF16))
        wo = _e(nc.sbuf_tensor("wo", [128, 8 * 1024], BF16))
        bout = _e(nc.sbuf_tensor("bout", [128, 1024], F32))
        gatesr = _e(nc.sbuf_tensor("gatesr", [128, 16], F32))
        identb = _e(nc.sbuf_tensor("identb", [128, 128], BF16))
        identf = _e(nc.sbuf_tensor("identf", [128, 128], F32))
        penal = _e(nc.sbuf_tensor("penal", [128, 128], F32))
        penalT = _e(nc.sbuf_tensor("penalT", [128, 128], F32))
        onescol = _e(nc.sbuf_tensor("onescol", [128, 1], F32))
        onesrow = _e(nc.sbuf_tensor("onesrow", [1, 128], F32))
        pall = _e(nc.psum_tensor("pall", [128, 3584], F32))
        pTb = _e(nc.psum_tensor("pTb", [128, 1024], BF16))
        s_sync = _e(nc.semaphore("sem_sync"))
        s_gp = _e(nc.semaphore("sem_gpsimd"))
        s_sc = _e(nc.semaphore("sem_scalar"))
        s_ve = _e(nc.semaphore("sem_vector"))
        s_te = _e(nc.semaphore("sem_tensor"))
        block = _e(nc.Block())
        sems = {"sync": s_sync, "gpsimd": s_gp, "scalar": s_sc,
                "vector": s_ve, "tensor": s_te}

        dma("sync", sim[:].rearrange("p (db e) -> p db e", db=8),
            AP(sim_in.tensor, 0, [[16, 128], [128 * 16, 8], [1, 16]]))
        dma("sync", gatesr[:], gates_in)
        dma("gpsimd", Ob[:].rearrange("p (e n) -> p e n", e=16),
            AP(o_in.tensor, 0, [[1024, 64], [64 * 1024, 16], [1, 1024]]))
        dma("gpsimd", wo[:].rearrange("p (db n) -> p db n", db=8),
            AP(woT_in.tensor, 0, [[1024, 128], [128 * 1024, 8], [1, 1024]]))
        dma("sync", bout[:], bout_in)
        dma("sync", identb[:], identb_in)
        dma("sync", identf[:], identf_in)
        dma("sync", penal[:], penal_in)
        dma("sync", penalT[:], penalT_in)
        dma("sync", onescol[:], onescol_in)
        dma("sync", onesrow[:], onesrow_in)
        E("scalar", lambda e: e.activation(out=sgrep[:], in_=gatesr[:], func=AF.Sigmoid))
        E("scalar", lambda e: e.activation(out=sim[:], in_=sim[:], func=AF.Square))
        for db in range(8):
            E("tensor", lambda e, db=db: e.matmul(pall[0:1, 0:16], onescol[:],
                                                  sim[:, db * 16:(db + 1) * 16],
                                                  start=(db == 0), stop=(db == 7)))
        E("vector", lambda e: e.tensor_copy(out=nrow[:], in_=pall[0:1, 0:16]))
        E("scalar", lambda e: e.activation(out=nrow[:], in_=nrow[:], func=AF.Sqrt))
        E("vector", lambda e: e.tensor_scalar(out=nrow[:], in0=nrow[:], scalar1=1e-12,
                                              scalar2=None, op0=ALU.max))
        E("vector", lambda e: e.reciprocal(out=nrow[:], in_=nrow[:]))
        E("tensor", lambda e: e.matmul(pall[:, 16:32], onesrow[:], nrow[:], start=True, stop=True))
        E("vector", lambda e: e.tensor_copy(out=normrep[:], in_=pall[:, 16:32]))
        dma("sync", sim[:].rearrange("p (db e) -> p db e", db=8),
            AP(sim_in.tensor, 0, [[16, 128], [128 * 16, 8], [1, 16]]))
        E("vector", lambda e: e.tensor_tensor(out=simn[:].rearrange("p (db e) -> p db e", db=8),
                                              in0=sim[:].rearrange("p (db e) -> p db e", db=8),
                                              in1=AP(normrep[:].tensor, 0, [[16, 128], [0, 8], [1, 16]]),
                                              op=ALU.mult))

        for b in range(2):
            for tc in range(32):
                dma("sync", xnc[:], AP(xn1b_in.tensor, (b * SEQ + tc * 128) * 1024,
                                       [[1024, 128], [1, 1024]]))
                for db in range(8):
                    E("tensor", lambda e, db=db: e.transpose(
                        out=pTb[:, db * 128:(db + 1) * 128],
                        in_=xnc[:, db * 128:(db + 1) * 128], identity=identb[:]))
                E("vector", lambda e, tc=tc: e.tensor_copy(
                    out=AP(arena[:].tensor, tc * 128, [[32768, 128], [4096, 8], [1, 128]]),
                    in_=pTb[:].rearrange("p (db t) -> p db t", db=8)))
                E("vector", lambda e: e.tensor_copy(out=xncf[:], in_=xnc[:]))
                E("scalar", lambda e: e.activation(out=xncf[:], in_=xncf[:], func=AF.Square))
                E("vector", lambda e: e.reduce_sum(out=colb[:, 0:1], in_=xncf[:], axis=AX.X))
                E("scalar", lambda e: e.activation(out=colb[:, 1:2], in_=colb[:, 0:1], func=AF.Sqrt))
                E("vector", lambda e: e.tensor_scalar(out=colb[:, 1:2], in0=colb[:, 1:2],
                                                      scalar1=1e-12, scalar2=None, op0=ALU.max))
                E("vector", lambda e: e.reciprocal(out=colb[:, 2:3], in_=colb[:, 1:2]))
                for db in range(8):
                    E("tensor", lambda e, db=db, tc=tc: e.matmul(
                        pall[:, 0:16],
                        AP(arena[:].tensor, db * 4096 + tc * 128, [[32768, 128], [1, 128]]),
                        simn[:, db * 16:(db + 1) * 16], start=(db == 0), stop=(db == 7)))
                E("vector", lambda e: e.tensor_scalar(out=lgt[:], in0=pall[:, 0:16],
                                                      scalar1=colb[:, 2:3], scalar2=None, op0=ALU.mult))
                E("vector", lambda e: e.tensor_tensor(out=lgt[:], in0=lgt[:], in1=sgrep[:],
                                                      op=ALU.subtract))
                E("vector", lambda e: e.reduce_max(out=colb[:, 3:4], in_=lgt[:], axis=AX.X))
                E("vector", lambda e: e.tensor_scalar(out=lgt2[:], in0=lgt[:], scalar1=colb[:, 3:4],
                                                      scalar2=None, op0=ALU.is_ge))
                E("vector", lambda e: e.scalar_tensor_tensor(out=fbm[:], in0=lgt2[:], scalar=-1e30,
                                                             in1=lgt[:], op0=ALU.mult, op1=ALU.add))
                E("vector", lambda e: e.reduce_max(out=colb[:, 4:5], in_=fbm[:], axis=AX.X))
                E("vector", lambda e: e.tensor_scalar(out=fbm[:], in0=lgt[:], scalar1=colb[:, 4:5],
                                                      scalar2=None, op0=ALU.is_ge))
                E("vector", lambda e: e.tensor_tensor(out=fbm[:], in0=fbm[:], in1=lgt2[:], op=ALU.max))
                E("vector", lambda e: e.tensor_relu(out=lgt[:], in_=lgt[:]))
                E("vector", lambda e: e.tensor_scalar(out=actm[:], in0=lgt[:], scalar1=0.0,
                                                      scalar2=None, op0=ALU.is_gt))
                E("vector", lambda e: e.reduce_sum(out=colb[:, 5:6], in_=actm[:], axis=AX.X))
                E("vector", lambda e: e.tensor_scalar(out=colb[:, 6:7], in0=colb[:, 5:6],
                                                      scalar1=0.0, scalar2=None, op0=ALU.is_equal))
                E("vector", lambda e: e.tensor_scalar(out=fbm[:], in0=fbm[:], scalar1=colb[:, 6:7],
                                                      scalar2=None, op0=ALU.mult))
                E("vector", lambda e: e.tensor_tensor(out=actm[:], in0=actm[:], in1=fbm[:], op=ALU.max))
                E("vector", lambda e: e.tensor_scalar(out=rwt[:], in0=actm[:], scalar1=-1.0,
                                                      scalar2=1.0, op0=ALU.mult, op1=ALU.add))
                E("vector", lambda e: e.scalar_tensor_tensor(out=rwt[:], in0=rwt[:], scalar=-1e30,
                                                             in1=lgt[:], op0=ALU.mult, op1=ALU.add))
                E("vector", lambda e: e.reduce_max(out=colb[:, 7:8], in_=rwt[:], axis=AX.X, negate=True))
                E("scalar", lambda e: e.activation(out=rwt[:], in_=rwt[:], func=AF.Exp,
                                                   bias=colb[:, 7:8]))
                E("vector", lambda e: e.reduce_sum(out=colb[:, 0:1], in_=rwt[:], axis=AX.X))
                E("vector", lambda e: e.reciprocal(out=colb[:, 1:2], in_=colb[:, 0:1]))
                E("vector", lambda e, b=b, tc=tc: e.tensor_scalar(
                    out=rwres[:, (b * 32 + tc) * 16:(b * 32 + tc + 1) * 16],
                    in0=rwt[:], scalar1=colb[:, 1:2], scalar2=None, op0=ALU.mult))
            E("vector", lambda e: e.memset(qkvacc[:], 0.0))
            for ee in range(16):
                dma("gpsimd", qkvw[:, (ee % 2) * 1536:(ee % 2) * 1536 + 1536]
                    .rearrange("p (db n) -> p db n", db=8),
                    AP(qkv_in.tensor, ee * 1024 * 192, [[192, 128], [128 * 192, 8], [1, 192]]))
                for tc in range(32):
                    for db in range(8):
                        E("tensor", lambda e, ee=ee, tc=tc, db=db: e.matmul(
                            pall[:, 512:704],
                            AP(arena[:].tensor, db * 4096 + tc * 128, [[32768, 128], [1, 128]]),
                            qkvw[:, (ee % 2) * 1536 + db * 192:(ee % 2) * 1536 + (db + 1) * 192],
                            start=(db == 0), stop=(db == 7)))
                    E("vector", lambda e, ee=ee, tc=tc, b=b: e.scalar_tensor_tensor(
                        out=qkvacc[:, tc * 192:(tc + 1) * 192],
                        in0=pall[:, 512:704],
                        scalar=rwres[:, (b * 32 + tc) * 16 + ee:(b * 32 + tc) * 16 + ee + 1],
                        in1=qkvacc[:, tc * 192:(tc + 1) * 192],
                        op0=ALU.mult, op1=ALU.add))
            E("vector", lambda e: e.memset(vbf[:], 0.0))
            E("vector", lambda e: e.memset(
                AP(vbf[:].tensor, 64, [[32 * 65, 128], [65, 32]]), 1.0))
            for tc in range(32):
                E("vector", lambda e, tc=tc: e.tensor_scalar(out=qsc[:], in0=qkvacc[:, tc * 192:tc * 192 + 64],
                                                             scalar1=0.125, scalar2=None, op0=ALU.mult))
                E("tensor", lambda e: e.transpose(out=pTb[0:64, 0:128], in_=qsc[:], identity=identb[:]))
                E("vector", lambda e, tc=tc: e.tensor_copy(out=qT[:, tc * 128:(tc + 1) * 128],
                                                           in_=pTb[0:64, 0:128]))
                E("vector", lambda e, tc=tc: e.tensor_copy(out=qsc[:],
                                                           in_=qkvacc[:, tc * 192 + 64:tc * 192 + 128]))
                E("tensor", lambda e: e.transpose(out=pTb[0:64, 128:256], in_=qsc[:], identity=identb[:]))
                E("vector", lambda e, tc=tc: e.tensor_copy(out=kT[:, tc * 128:(tc + 1) * 128],
                                                           in_=pTb[0:64, 128:256]))
                E("vector", lambda e, tc=tc: e.tensor_copy(out=vbf[:, tc * 65:tc * 65 + 64],
                                                           in_=qkvacc[:, tc * 192 + 128:tc * 192 + 192]))
            for db in range(8):
                dma("sync", arena[:, db * 4096:(db + 1) * 4096],
                    AP(v2_in.tensor, (b * 1024 + db * 128) * SEQ, [[SEQ, 128], [1, SEQ]]))
            for qtc in range(32):
                E("vector", lambda e: e.memset(runm[:], -1e30))
                for ktc in range(qtc + 1):
                    E("tensor", lambda e, qtc=qtc, ktc=ktc: e.matmul(
                        pall[:, 1024:1152], qT[:, qtc * 128:(qtc + 1) * 128],
                        kT[:, ktc * 128:(ktc + 1) * 128], start=True, stop=True))
                    if ktc == qtc:
                        E("vector", lambda e: e.tensor_tensor(out=stf[:], in0=pall[:, 1024:1152],
                                                              in1=penal[:], op=ALU.add))
                        E("vector", lambda e: e.reduce_max(out=rmt[:], in_=stf[:], axis=AX.X))
                    else:
                        E("vector", lambda e: e.reduce_max(out=rmt[:], in_=pall[:, 1024:1152], axis=AX.X))
                    E("vector", lambda e: e.tensor_tensor(out=runm[:], in0=runm[:], in1=rmt[:], op=ALU.max))
                E("vector", lambda e: e.tensor_scalar(out=rmt[:], in0=runm[:], scalar1=-1.0,
                                                      scalar2=None, op0=ALU.mult))
                E("tensor", lambda e: e.transpose(out=pall[0:1, 1536:1664], in_=rmt[:],
                                                  identity=identf[:]))
                E("vector", lambda e: e.tensor_copy(out=nmrow[:], in_=pall[0:1, 1536:1664]))
                E("tensor", lambda e: e.matmul(pall[:, 1536:1664], onesrow[:], nmrow[:],
                                               start=True, stop=True))
                E("vector", lambda e: e.tensor_copy(out=nmrep[:], in_=pall[:, 1536:1664]))
                for ktc in range(qtc + 1):
                    E("tensor", lambda e, qtc=qtc, ktc=ktc: e.matmul(
                        pall[:, 1024:1152], kT[:, ktc * 128:(ktc + 1) * 128],
                        qT[:, qtc * 128:(qtc + 1) * 128], start=True, stop=True))
                    E("vector", lambda e: e.tensor_tensor(out=stf[:], in0=pall[:, 1024:1152],
                                                          in1=nmrep[:], op=ALU.add))
                    if ktc == qtc:
                        E("vector", lambda e: e.tensor_tensor(out=stf[:], in0=stf[:],
                                                              in1=penalT[:], op=ALU.add))
                    E("scalar", lambda e: e.activation(out=ptb[:], in_=stf[:], func=AF.Exp))
                    E("tensor", lambda e, ktc=ktc, qtc=qtc: e.matmul(
                        pall[:, 1664:1729], ptb[:],
                        vbf[:, ktc * 65:(ktc + 1) * 65],
                        start=(ktc == 0), stop=(ktc == qtc)))
                E("vector", lambda e: e.reciprocal(out=rcol[:], in_=pall[:, 1728:1729]))
                E("vector", lambda e: e.tensor_scalar(out=attb[:], in0=pall[:, 1664:1728],
                                                      scalar1=rcol[:, 0:1], scalar2=None, op0=ALU.mult))
                E("tensor", lambda e: e.transpose(out=pTb[0:64, 0:128], in_=attb[:], identity=identb[:]))
                E("vector", lambda e: e.tensor_copy(out=attT[:], in_=pTb[0:64, 0:128]))
                E("vector", lambda e: e.memset(oacc[:], 0.0))
                for ee in range(16):
                    for ch in range(2):
                        E("tensor", lambda e, ee=ee, ch=ch: e.matmul(
                            pall[:, 2048 + ch * 512:2048 + (ch + 1) * 512],
                            attT[:], Ob[:, ee * 1024 + ch * 512:ee * 1024 + (ch + 1) * 512],
                            start=True, stop=True))
                        E("vector", lambda e, ee=ee, ch=ch, b=b, qtc=qtc: e.scalar_tensor_tensor(
                            out=oacc[:, ch * 512:(ch + 1) * 512],
                            in0=pall[:, 2048 + ch * 512:2048 + (ch + 1) * 512],
                            scalar=rwres[:, (b * 32 + qtc) * 16 + ee:(b * 32 + qtc) * 16 + ee + 1],
                            in1=oacc[:, ch * 512:(ch + 1) * 512],
                            op0=ALU.mult, op1=ALU.add))
                for ch in range(2):
                    for db in range(8):
                        E("tensor", lambda e, ch=ch, db=db, qtc=qtc: e.matmul(
                            pall[:, ch * 512:(ch + 1) * 512],
                            AP(arena[:].tensor, db * 4096 + qtc * 128, [[32768, 128], [1, 128]]),
                            wo[:, db * 1024 + ch * 512:db * 1024 + (ch + 1) * 512],
                            start=(db == 0), stop=(db == 7)))
                dma("sync", x1t[:], AP(x1_in.tensor, (b * SEQ + qtc * 128) * 1024,
                                       [[1024, 128], [1, 1024]]))
                for ch in range(2):
                    E("vector", lambda e, ch=ch: e.tensor_tensor(
                        out=outt[:, ch * 512:(ch + 1) * 512], in0=pall[:, ch * 512:(ch + 1) * 512],
                        in1=oacc[:, ch * 512:(ch + 1) * 512], op=ALU.add))
                    E("vector", lambda e, ch=ch: e.tensor_tensor(
                        out=outt[:, ch * 512:(ch + 1) * 512], in0=outt[:, ch * 512:(ch + 1) * 512],
                        in1=x1t[:, ch * 512:(ch + 1) * 512], op=ALU.add))
                    E("vector", lambda e, ch=ch: e.tensor_tensor(
                        out=outt[:, ch * 512:(ch + 1) * 512], in0=outt[:, ch * 512:(ch + 1) * 512],
                        in1=bout[:, ch * 512:(ch + 1) * 512], op=ALU.add))
                dma("sync", AP(out_t.tensor, (b * SEQ + qtc * 128) * 1024,
                               [[1024, 128], [1, 1024]]), outt[:])
        sb.attach(nc, block, sems)
    return nc


def d_host_inputs(core, inputs, x1, xn1b, v2):
    import ml_dtypes
    qkv = np.concatenate([inputs["L1_ffn_q_proj"], inputs["L1_ffn_k_proj"],
                          inputs["L1_ffn_v_proj"]], axis=2)
    penal = np.where(np.arange(128)[None, :] > np.arange(128)[:, None], -1e9, 0.0).astype(np.float32)
    penalT = np.where(np.arange(128)[:, None] > np.arange(128)[None, :], -1e9, 0.0).astype(np.float32)
    return {
        "x1_in": x1,
        "xn1b_in": xn1b,
        "v2_in": v2,
        "sim_in": np.ascontiguousarray(inputs["L1_ffn_sim_matrix"]),
        "gates_in": np.ascontiguousarray(np.broadcast_to(inputs["L1_ffn_gates"][None, :], (128, 16))),
        "qkv_in": np.ascontiguousarray(qkv),
        "o_in": np.ascontiguousarray(inputs["L1_ffn_o_proj"]),
        "woT_in": np.ascontiguousarray(inputs["L1_attn_w_out"].T),
        "bout_in": np.ascontiguousarray(np.broadcast_to(inputs["L1_attn_b_out"][None, :], (128, 1024))),
        "identb_in": np.eye(128, dtype=np.float32).astype(ml_dtypes.bfloat16),
        "identf_in": np.eye(128, dtype=np.float32),
        "penal_in": penal,
        "penalT_in": penalT,
        "onescol_in": np.ones((128, 1), np.float32),
        "onesrow_in": np.ones((1, 128), np.float32),
    }


def kernel(**inputs):
    inputs = {k: np.asarray(v) for k, v in inputs.items()}
    cores = list(range(8))

    ncA1 = build_A1()
    resA1 = run_bass_kernel_spmd(ncA1, [a1_host_inputs(c, inputs) for c in cores], core_ids=cores)
    prevs = resA1.results

    ncA2 = build_A2()
    resA2 = run_bass_kernel_spmd(ncA2, [a2_host_inputs(c, inputs, prevs[c]) for c in cores],
                                 core_ids=cores)
    x1s = [r["x1"] for r in resA2.results]

    ncB = build_B()
    resB = run_bass_kernel_spmd(ncB, [b_host_inputs(c, inputs, x1s[c]) for c in cores],
                                core_ids=cores)
    import ml_dtypes
    X_full = np.empty((3072, 16, SEQ), dtype=ml_dtypes.bfloat16)
    for c in cores:
        X_full[:, 2 * c:2 * c + 2, :] = resB.results[c]["X"].reshape(3072, 2, SEQ)

    ncC = build_C()
    resC = run_bass_kernel_spmd(ncC, [c_host_inputs(c, inputs, X_full) for c in cores],
                                core_ids=cores)
    v2_full = np.empty((1024, 16, SEQ), dtype=ml_dtypes.bfloat16)
    for c in cores:
        v2_full[c * 128:(c + 1) * 128] = resC.results[c]["Y"].reshape(128, 16, SEQ)

    ncD = build_D()
    mapsD = []
    for c in cores:
        v2c = np.ascontiguousarray(v2_full[:, 2 * c:2 * c + 2, :].transpose(1, 0, 2)).reshape(-1)
        mapsD.append(d_host_inputs(c, inputs, x1s[c], resB.results[c]["xn1b"], v2c))
    resD = run_bass_kernel_spmd(ncD, mapsD, core_ids=cores)

    out = np.concatenate([r["out_t"].reshape(2, SEQ, 1024) for r in resD.results], axis=0)
    return out
